# version 01
# baseline (speedup 1.0000x reference)
import math
import numpy as np
import ml_dtypes
import concourse.bass as bass
import concourse.mybir as mybir
from concourse.bass_utils import run_bass_kernel_spmd

F32 = mybir.dt.float32
BF16 = mybir.dt.bfloat16
AF = mybir.ActivationFunctionType
AX = mybir.AxisListType
OP = mybir.AluOpType

NCORE = 8
D = 1024
SEQ = 2048
NSEQ = 2
T = SEQ * NSEQ
NC = D // 128
TT128 = T // 128
TT512 = T // 512
NEG = -1e30

BF = ml_dtypes.bfloat16


class Tok:
    __slots__ = ("inst", "eng", "val", "dma")
    def __init__(self, inst, eng, dma=False):
        self.inst = inst
        self.eng = eng
        self.val = None
        self.dma = dma


class Prog:

    ENGINES = ("tensor", "vector", "scalar", "gpsimd", "sync")
    NDPOOL = 64

    def __init__(self):
        self.nc = bass.Bass("TRN2", target_bir_lowering=False)
        self.sem = {}
        self.count = {}
        self.seen = {e: {} for e in self.ENGINES}
        for e in self.ENGINES:
            self.sem[e] = self.nc.alloc_semaphore(f"s_{e}")
            self.count[e] = 0
        for i in range(self.NDPOOL):
            k = f"dp{i}"
            self.sem[k] = self.nc.alloc_semaphore(k)
            self.count[k] = 0
        self._dpnext = 0
        self._uid = 0

    def dgroup(self):
        k = f"dp{self._dpnext % self.NDPOOL}"
        self._dpnext += 1
        return k

    def uid(self):
        self._uid += 1
        return self._uid

    def _eng(self, e):
        return getattr(self.nc, e)

    def wait(self, consumer, *toks):
        eng = self._eng(consumer)
        for tok in toks:
            if tok is None:
                continue
            assert tok.val is not None, "cannot depend on a non-stop matmul"
            v = tok.val
            seen = self.seen[consumer]
            if seen.get(tok.eng, 0) < v:
                eng.wait_ge(self.sem[tok.eng], v)
                seen[tok.eng] = v

    def op(self, engine, fname, *args, deps=(), force_tok=False, **kw):
        self.wait(engine, *deps)
        inst = getattr(self._eng(engine), fname)(*args, **kw)
        tok = Tok(inst, engine)
        if force_tok or not (engine == "tensor" and fname == "matmul"
                             and kw.get("stop") is False):
            self.count[engine] += 1
            tok.val = self.count[engine]
            inst.then_inc(self.sem[engine], 1)
        return tok

    def dma(self, queue, out, in_, deps=(), grp=None, **kw):
        self.wait(queue, *deps)
        inst = self._eng(queue).dma_start(out=out, in_=in_, **kw)
        if grp is None:
            grp = self.dgroup()
        self.count[grp] += 16
        inst.then_inc(self.sem[grp], 16)
        tok = Tok(inst, grp, dma=True)
        tok.val = self.count[grp]
        return tok

    def barrier(self):
        self.nc.all_engine_barrier()
        for e in self.ENGINES:
            for k in self.ENGINES:
                if self.count[k] > self.seen[e].get(k, 0):
                    self.seen[e][k] = self.count[k]

    def fence(self):
        for i in range(self.NDPOOL):
            q = f"dp{i}"
            if self.count[q] > self.seen["sync"].get(q, 0):
                self.nc.sync.wait_ge(self.sem[q], self.count[q])
                self.seen["sync"][q] = self.count[q]
        self.barrier()
        for e in self.ENGINES:
            for i in range(self.NDPOOL):
                q = f"dp{i}"
                if self.count[q] > self.seen[e].get(q, 0):
                    self.seen[e][q] = self.count[q]

    def sbuf(self, name, shape, dtype):
        return self.nc.alloc_sbuf_tensor(name, shape, dtype)

    def dram(self, name, shape, dtype, kind="Internal"):
        return self.nc.dram_tensor(name, shape, dtype, kind=kind)

    def psum(self, name, shape, dtype=F32):
        return self.nc.alloc_psum_tensor(name, shape, dtype)


def _fm(w):
    K, N = w.shape
    assert K % 128 == 0, (K, N)
    return np.ascontiguousarray(w.reshape(K // 128, 128, N).astype(np.float32))


def prep_inputs(inp):
    g = {k: np.asarray(v) for k, v in inp.items()}
    shared = {}

    x = g["L0_attn_input_states"]
    percore = []
    for c in range(NCORE):
        xc = x[2 * c:2 * c + 2].reshape(T, D).T
        percore.append({"x_in": np.ascontiguousarray(xc.reshape(NC, 128, T))})

    shared["c_ones"] = np.ones((128, 128), dtype=BF)
    shared["c_ident"] = np.eye(128, dtype=np.float32).astype(BF)
    shared["c_identf"] = np.eye(128, dtype=np.float32)
    ltri = np.where(np.arange(128)[:, None] > np.arange(128)[None, :], NEG, 0.0)
    shared["c_ltri_neg"] = ltri.astype(np.float32)
    ii = np.arange(128)[:, None]
    jj = np.arange(512)[None, :]
    for sh in range(4):
        shared[f"c_cmask{sh}"] = np.where(ii + sh * 128 > jj, NEG, 0.0).astype(BF)
    return percore, shared


class Ctx:
    pass


def constcol(cx, value, dtype=F32):
    key = (float(value), dtype)
    if key in cx.constcols:
        return cx.constcols[key]
    tile = cx.p.sbuf(f"cc_{len(cx.constcols)}", [128, 1], dtype)
    cx.p.op("vector", "memset", tile[:], float(value))
    cx.p.barrier()
    cx.constcols[key] = tile
    return tile


def build_common(p: Prog):
    cx = Ctx()
    cx.p = p
    cx.constcols = {}
    cx.d = {}
    cx.xn = p.sbuf("xn_sb", [128, NC * T], BF16)
    cx.ones = p.sbuf("c_ones_sb", [128, 128], BF16)
    cx.ident = p.sbuf("c_ident_sb", [128, 128], BF16)
    cx.identf = p.sbuf("c_identf_sb", [128, 128], F32)
    cx.ltri = p.sbuf("c_ltri_sb", [128, 128], F32)
    cx.stage = p.sbuf("stage_f32", [128, 4096], F32)
    cx.stage2 = p.sbuf("stage2_f32", [128, 2048], F32)
    cx.bbA = p.sbuf("bbA", [128, 4096], BF16)
    cx.bbB = p.sbuf("bbB", [128, 4096], BF16)
    cx.f512 = [p.sbuf(f"f512_{i}", [128, 512], F32) for i in range(5)]
    _wa = p.sbuf("warena", [128, 16384], BF16)
    cx.wres = _wa.ap()
    _waf = _wa.ap().bitcast(F32)
    cx.wfs = [_waf[:, 0:1024], _waf[:, 1024:2048]]
    cx.wb = [_wa.ap()[:, 8192:11008], _wa.ap()[:, 11008:13824]]
    cx.att_q = [p.sbuf(f"attq{i}", [128, SEQ], BF16) for i in range(2)]
    cx.att_k = [p.sbuf(f"attk{i}", [128, SEQ], BF16) for i in range(2)]
    cx.att_v = [p.sbuf(f"attv{i}", [128, 1024], BF16) for i in range(2)]
    cx.att_p = [p.sbuf(f"attp{i}", [128, 512], BF16) for i in range(3)]
    cx.att_o = p.sbuf("atto", [128, 512], BF16)
    cx.cos = p.sbuf("cos_sb", [128, SEQ], BF16)
    cx.sin = p.sbuf("sin_sb", [128, SEQ], BF16)
    cx.cmask = [p.sbuf(f"cmask{i}", [128, 512], BF16) for i in range(4)]
    cx.evout = p.sbuf("evout", [128, 4096], BF16)
    cx.brow = p.sbuf("brow", [128, 1024], F32)
    cx.cw = p.sbuf("cw_sb", [128, 512], BF16)
    cx.ccols = p.sbuf("ccols", [128, 96], F32)
    cx.acsT = p.sbuf("acsT", [128, 1024], F32)
    cx.bcols = p.sbuf("bcols", [128, 32], F32)
    cx.x_hbm = p.dram("x_hbm", [NC, 128, T], F32)
    cx.scrA = p.dram("scrA", [35, 128, T], BF16)
    cx.scrB = p.dram("scrB", [32, 128, 1024], BF16)
    cx.scrC = p.dram("scrC", [16, 128, T], BF16)
    cx.scrD = p.dram("scrD", [TT128, 128, 2048], BF16)
    cx.acs_hbm = p.dram("acs_hbm", [32, T], F32)
    cx.scrD2 = p.dram("scrD2", [TT128, 128, 2048], BF16)
    cx.s_hbm = p.dram("s_hbm", [NC, 128, T], BF16)
    cx.g_hbm = p.dram("g_hbm", [NC, 128, T], BF16)
    cx.pb = [p.psum(f"pb{i}", [128, 512], F32) for i in range(8)]
    cx.pbbf = [t.ap().bitcast(BF16) for t in cx.pb]
    cx.pbm = cx.pb[7]
    return cx


def load_consts(cx, consts_aps):
    p = cx.p
    toks = [p.dma("sync", cx.ones[:], consts_aps["c_ones"]),
            p.dma("sync", cx.ident[:], consts_aps["c_ident"]),
            p.dma("sync", cx.identf[:], consts_aps["c_identf"]),
            p.dma("sync", cx.ltri[:], consts_aps["c_ltri_neg"])]
    for i in range(4):
        if f"c_cmask{i}" in consts_aps:
            toks.append(p.dma("sync", cx.cmask[i][:], consts_aps[f"c_cmask{i}"]))
    p.wait("sync", *toks)
    p.fence()


def emit_ln(cx, src_hbm, dst_xn, eps=1e-5):
    p = cx.p
    xs = cx.stage
    xb = cx.bbA
    xq = cx.bbB
    mu, rs, tv = cx.f512[0], cx.f512[1], cx.f512[2]
    eps_col = constcol(cx, eps)
    prev = [None, None]
    for t in range(TT512):
        t0 = t * 512
        ldt = [p.dma("sync", xs[:, c * 512:(c + 1) * 512],
                     src_hbm.ap()[c, :, t0:t0 + 512], deps=prev)
               for c in range(NC)]
        casts, sqs = [], []
        for c in range(NC):
            sl = slice(c * 512, (c + 1) * 512)
            casts.append(p.op("vector", "tensor_copy", xb[:, sl], xs[:, sl],
                              deps=[ldt[c]]))
            sqs.append(p.op("scalar", "activation", xq[:, sl], xs[:, sl],
                            AF.Square, deps=[ldt[c]]))
        ps_s, ps_q = cx.pb[0], cx.pb[1]
        mm_s = mm_q = None
        for c in range(NC):
            sl = slice(c * 512, (c + 1) * 512)
            mm_s = p.op("tensor", "matmul", ps_s[:], cx.ones[:], xb[:, sl],
                        start=(c == 0), stop=(c == NC - 1), deps=[casts[c]])
            mm_q = p.op("tensor", "matmul", ps_q[:], cx.ones[:], xq[:, sl],
                        start=(c == 0), stop=(c == NC - 1), deps=[sqs[c]])
        t1 = p.op("vector", "tensor_scalar_mul", mu[:], ps_s[:], 1.0 / D, deps=[mm_s])
        t2 = p.op("vector", "tensor_tensor", tv[:], mu[:], mu[:], OP.mult, deps=[t1])
        t3 = p.op("vector", "scalar_tensor_tensor", tv[:], ps_q[:], 1.0 / D, tv[:],
                  OP.mult, OP.subtract, deps=[mm_q, t2])
        t4 = p.op("scalar", "activation", rs[:], tv[:], AF.Ln, bias=eps_col[:],
                  deps=[t3])
        t5 = p.op("scalar", "activation", rs[:], rs[:], AF.Exp, scale=-0.5, deps=[t4])
        prev = []
        for c in range(NC):
            sl = slice(c * 512, (c + 1) * 512)
            eng = "vector" if c % 2 == 0 else "gpsimd"
            u1 = p.op(eng, "tensor_tensor", xb[:, sl], xs[:, sl], mu[:],
                      OP.subtract, deps=[t1, casts[c], sqs[c], mm_s, mm_q])
            dsl = dst_xn[:, c * T + t0: c * T + t0 + 512]
            u2 = p.op(eng, "tensor_tensor", dsl, xb[:, sl], rs[:], OP.mult,
                      deps=[t5, u1])
            prev.append(u2)
    p.fence()


def emit_proj_fm2(cx, w_hbm, act_sb, K, N, evac):
    p = cx.p
    KC, NCo = K // 128, N // 128
    assert KC * 128 <= 2816
    last_mm = [None, None]
    ev_dep = [[None] * TT512 for _ in range(2)]
    stg_guard = [None, None]
    for o in range(NCo):
        bi = o % 2
        wbb = cx.wb[bi]
        npc = (KC * 128 + 1023) // 1024
        casts = []
        for pc in range(npc):
            c0, c1 = pc * 1024, min(KC * 128, pc * 1024 + 1024)
            k0, k1 = c0 // 128, (c1 + 127) // 128
            sb = cx.wfs[pc % 2]
            ld = p.dma("sync", sb[:, 0:c1 - c0].rearrange("p (k c) -> p k c", c=128),
                       w_hbm.ap()[k0:k1, :, o * 128:(o + 1) * 128]
                       .rearrange("k p c -> p k c"),
                       deps=[stg_guard[pc % 2], last_mm[bi]])
            ck = p.op("vector", "tensor_copy", wbb[:, c0:c1], sb[:, 0:c1 - c0],
                      deps=[ld])
            stg_guard[pc % 2] = ck
            casts.append(ck)
        mms = [None] * TT512
        for k in range(KC):
            for t in range(TT512):
                dep = casts + [ev_dep[bi][t]] if k == 0 else []
                mms[t] = p.op("tensor", "matmul", cx.pb[t][:],
                              wbb[:, k * 128:(k + 1) * 128],
                              act_sb[:, k * T + t * 512: k * T + (t + 1) * 512],
                              start=(k == 0), stop=(k == KC - 1), deps=dep)
        last_mm[bi] = mms[TT512 - 1]
        for t in range(TT512):
            ev_dep[bi][t] = evac(o, t, mms[t], bank=t)
    p.fence()


def make_rope_evac(cx, out_hbm, cos_sb, sin_sb, bias_sb=None, dram_off=0,
                   swap_block=32):
    p = cx.p
    swg = [None] * 4
    ybg = [None] * 8
    t1g = [None] * 4
    outg = [None] * 8

    def evac(o, t, mtok, bank):
        ps = cx.pb[bank]
        pos = (t * 512) % SEQ
        i4, i8 = t % 4, (o * TT512 + t) % 8
        yb = cx.bbB[:, i8 * 512:i8 * 512 + 512]
        sw = cx.bbA[:, i4 * 512:i4 * 512 + 512]
        t1 = cx.stage2[:, i4 * 512:i4 * 512 + 512]
        out = cx.evout[:, i8 * 512:i8 * 512 + 512]
        if bias_sb is not None:
            ev = p.op("scalar", "activation", yb, ps[:], AF.Identity,
                      bias=bias_sb[:, o:o + 1], deps=[mtok, ybg[i8]])
        else:
            ev = p.op("scalar", "copy", yb, ps[:], deps=[mtok, ybg[i8]])
        cps = []
        nblk = 128 // swap_block
        for blk in range(nblk):
            d0 = blk * swap_block
            s0 = (blk ^ 1) * swap_block
            eng = "vector" if blk % 2 == 0 else "gpsimd"
            cps.append(p.op(eng, "tensor_copy", sw[d0:d0 + swap_block, :],
                            yb[s0:s0 + swap_block, :],
                            deps=[ev] + ([swg[i4]] if blk < 2 else [])))
        m1 = p.op("vector", "tensor_tensor", t1[:], yb[:],
                  cos_sb[:, pos:pos + 512], OP.mult, deps=[ev, t1g[i4]])
        a1 = p.op("gpsimd", "tensor_tensor", sw[:], sw[:],
                  sin_sb[:, pos:pos + 512], OP.mult, deps=cps)
        a2 = p.op("vector", "tensor_tensor", out, t1[:], sw[:], OP.add,
                  deps=[m1, a1, outg[i8]])
        swg[i4] = a2
        t1g[i4] = a2
        st = p.dma("sync", out_hbm.ap()[dram_off + o, :, t * 512:(t + 1) * 512],
                   out, deps=[a2])
        outg[i8] = st
        ybg[i8] = a2
        return ev

    return evac


def make_bias_evac(cx, out_hbm, bias_sb=None, act_fn=None, dram_off=0, scale=1.0):
    p = cx.p
    outg = [None] * 8

    def evac(o, t, mtok, bank):
        i8 = (o + t) % 8
        out = cx.evout[:, i8 * 512:i8 * 512 + 512]
        fn = act_fn if act_fn is not None else (
            AF.Identity if bias_sb is not None else AF.Copy)
        b = bias_sb[:, o:o + 1] if bias_sb is not None else 0.0
        ev = p.op("scalar", "activation", out, cx.pb[bank][:], fn, bias=b,
                  scale=scale, deps=[mtok, outg[i8]])
        st = p.dma("sync", out_hbm.ap()[dram_off + o, :, t * 512:(t + 1) * 512],
                   out, deps=[ev])
        outg[i8] = st
        return ev

    return evac


def emit_proj_tm(cx, w_hbm, act_sb, K, N, out_hbm, bias_row=None, dram_voff=0,
                 act_hbm=None, act_off=0, col_off=0, col_lim=None):
    p = cx.p
    KC = K // 128
    NB = (N + 511) // 512
    assert N <= 1024 and KC * N <= 16384
    wres = cx.wres
    for k in range(KC):
        ld = p.dma("sync", cx.stage[:, 0:N], w_hbm.ap()[k], deps=[])
        ck = p.op("vector", "tensor_copy", wres[:, k * N:(k + 1) * N],
                  cx.stage[:, 0:N], deps=[ld])
        p.wait("sync", ck)
    p.barrier()
    evd = [None, None]
    actg = [None, None]
    for tt in range(TT128):
        t0 = tt * 128
        bi = tt % 2
        ps = cx.pb[bi * 2], cx.pb[bi * 2 + 1]
        if act_hbm is not None:
            ast = cx.bbA[:, bi * 2048:bi * 2048 + KC * 128]
            lda = p.dma("sync", ast.rearrange("p (k c) -> p k c", c=128),
                        act_hbm.ap()[act_off:act_off + KC, :, t0:t0 + 128]
                        .rearrange("k p c -> p k c"), deps=[actg[bi]])
        mms = [None] * NB
        for k in range(KC):
            if act_hbm is not None:
                lhs = ast[:, k * 128:(k + 1) * 128]
                kdep = [lda]
            else:
                lhs = act_sb[:, k * T + t0: k * T + t0 + 128]
                kdep = []
            for nb in range(NB):
                w0 = nb * 512
                w1 = min(N, w0 + 512)
                dep = list(kdep)
                if k == 0:
                    dep.append(evd[bi])
                mms[nb] = p.op("tensor", "matmul", ps[nb][:, 0:w1 - w0],
                               lhs, wres[:, k * N + w0: k * N + w1],
                               start=(k == 0), stop=(k == KC - 1), deps=dep)
        if act_hbm is not None:
            actg[bi] = mms[0]
        out = cx.evout[:, bi * 1024: bi * 1024 + N]
        evs = []
        for nb in range(NB):
            w0 = nb * 512
            w1 = min(N, w0 + 512)
            if bias_row is not None:
                ev = p.op("vector", "tensor_tensor", out[:, w0:w1],
                          ps[nb][:, 0:w1 - w0], bias_row[:, w0:w1], OP.add,
                          deps=[mms[nb]])
            else:
                ev = p.op("vector", "tensor_copy", out[:, w0:w1],
                          ps[nb][:, 0:w1 - w0], deps=[mms[nb]])
            evs.append(ev)
        evd[bi] = p.dma("scalar",
                        out_hbm.ap()[dram_voff + tt, :, col_off:col_off + N],
                        out, deps=evs)
    p.fence()


def emit_proj_hbm_act(cx, w_hbm, act_hbm, K, N, evac, act_off=0,
                      w_bf16=False):
    p = cx.p
    KC, NCo = K // 128, N // 128
    assert KC * N <= 16384
    wres = cx.wres
    for k in range(KC):
        if w_bf16:
            ld = p.dma("sync", wres[:, k * N:(k + 1) * N], w_hbm.ap()[k])
            p.wait("sync", ld)
        else:
            ld = p.dma("sync", cx.stage[:, 0:N], w_hbm.ap()[k], deps=[])
            ck = p.op("vector", "tensor_copy", wres[:, k * N:(k + 1) * N],
                      cx.stage[:, 0:N], deps=[ld])
            p.wait("sync", ck)
    p.barrier()
    act_guard = [None, None]
    ev_dep = [None] * 8
    stageb = cx.stage.ap().bitcast(BF16)
    NKB = (KC + 7) // 8
    for t in range(TT512):
        bi = t % 2
        last = None
        for kb in range(NKB):
            k0, k1 = kb * 8, min(KC, kb * 8 + 8)
            act = stageb[:, bi * 4096:bi * 4096 + 4096]
            lds = [p.dma("sync", act[:, (k - k0) * 512:(k - k0 + 1) * 512],
                         act_hbm.ap()[act_off + k, :, t * 512:(t + 1) * 512],
                         deps=[act_guard[bi]]) for k in range(k0, k1)]
            for o in range(NCo):
                mm = None
                for k in range(k0, k1):
                    dep = [lds[k - k0]]
                    if k == k0:
                        dep = list(lds)
                        if kb == 0:
                            dep.append(ev_dep[o])
                    mm = p.op("tensor", "matmul", cx.pb[o][:],
                              wres[:, k * N + o * 128: k * N + (o + 1) * 128],
                              act[:, (k - k0) * 512:(k - k0 + 1) * 512],
                              start=(k == 0 and kb == 0),
                              stop=(k == KC - 1), deps=dep,
                              force_tok=(k == k1 - 1 and o == NCo - 1))
                if kb == NKB - 1:
                    ev_dep[o] = evac(o, t, mm, bank=o)
                last = mm
            act_guard[bi] = last
    p.fence()


def emit_attention(cx, q_hbm, k_hbm, v_hbm, out_hbm, nheads=16, causal=False,
                   scale=0.125, q_off=0, k_off=0, out_off=0, heads=None,
                   dbg_r=None, dbg_av=None, dbg_p=None):
    p = cx.p
    NKT = SEQ // 128
    NQB = SEQ // 512
    NQT = SEQ // 128
    aq, ak, av = cx.att_q, cx.att_k, cx.att_v
    heads = heads if heads is not None else [(s, h) for s in range(NSEQ)
                                            for h in range(nheads)]
    mq = cx.f512[3]
    rin = cx.f512[4]
    bufg = [{}, {}]
    bankg = {}
    mtrg = None
    mqg = None
    dbgg = [None]
    dbgp_g = [None, None, None]
    for idx, (s, h) in enumerate(heads):
        bi = idx % 2
        soff = s * SEQ
        ch, r0 = h // 2, (h % 2) * 64
        g = bufg[bi]
        lq = p.dma("sync", aq[bi][0:64, :],
                   q_hbm.ap()[q_off + ch, r0:r0 + 64, soff:soff + SEQ],
                   deps=[g.get("lastmm"), g.get("lastv")])
        lk = p.dma("sync", ak[bi][0:64, :],
                   k_hbm.ap()[k_off + ch, r0:r0 + 64, soff:soff + SEQ],
                   deps=[g.get("lastmm")])
        lv = p.dma("scalar", av[bi][:].rearrange("p (tt j) -> p tt j", j=64),
                   v_hbm.ap().rearrange("tt p n -> p tt n")[
                       :, s * NKT:(s + 1) * NKT, h * 64:(h + 1) * 64],
                   deps=[g.get("lastmm")])
        on = p.op("vector", "memset", ak[bi][64:65, :], 1.0,
                  deps=[g.get("lastmm"), lk])
        mrow = []
        for qt in range(NQT):
            q0 = qt * 128
            nkb = (q0 // 512 + 1) if causal else NQB
            red = []
            for kb in range(nkb):
                bank = kb % 2
                mm = p.op("tensor", "matmul", cx.pb[bank][:],
                          aq[bi][0:64, q0:q0 + 128],
                          ak[bi][0:64, kb * 512:(kb + 1) * 512],
                          start=True, stop=True,
                          deps=[lq, lk, bankg.get(bank)])
                rt = p.op("vector", "tensor_reduce", mq[:, 1 + kb:2 + kb],
                          cx.pb[bank][:], axis=AX.X, op=OP.max,
                          deps=[mm, mqg])
                bankg[bank] = rt
                red.append(rt)
            cm = p.op("vector", "tensor_reduce", mq[:, 0:1], mq[:, 1:1 + nkb],
                      axis=AX.X, op=OP.max, deps=red)
            mqg = cm
            tr = p.op("tensor", "transpose", cx.pbm[0:1, 0:128], mq[:, 0:1],
                      cx.identf[:], deps=[cm, mtrg])
            mr = p.op("vector", "tensor_scalar_mul",
                      aq[bi][64:65, q0:q0 + 128], cx.pbm[0:1, 0:128], -1.0,
                      deps=[tr])
            mtrg = mr
            mrow.append(mr)
        last_st = None
        for qb in range(NQB):
            qb0 = qb * 512
            kts = list(range(min(NKT, (qb0 + 512) // 128) if causal else NKT))
            rmm = avmm = None
            for j, kt in enumerate(kts):
                k0 = kt * 128
                bank = 2 + kt % 3
                pt = cx.att_p[kt % 3]
                stm = p.op("tensor", "matmul", cx.pb[bank][:],
                           ak[bi][0:65, k0:k0 + 128],
                           aq[bi][0:65, qb0:qb0 + 512],
                           start=True, stop=True,
                           deps=[on] + mrow[4 * qb:4 * qb + 4] +
                                [bankg.get(bank)])
                last_st = stm
                dep_mask = []
                if causal and k0 >= qb0:
                    sh = (k0 - qb0) // 128
                    mk = p.op("vector", "tensor_tensor", cx.pb[bank][:],
                              cx.pb[bank][:], cx.cmask[sh][:], OP.add,
                              deps=[stm])
                    dep_mask = [mk]
                ex = p.op("scalar", "activation", pt[:], cx.pb[bank][:], AF.Exp,
                          scale=scale, deps=[stm, dbgp_g[kt % 3]] + dep_mask)
                if dbg_p is not None and qb == 0 and idx == 0:
                    dbgp_g[kt % 3] = p.dma("sync", dbg_p.ap()[kt], pt[:],
                                           deps=[ex])
                bankg[bank] = ex
                rmm = p.op("tensor", "matmul", cx.pb[5][:], cx.ones[:], pt[:],
                           start=(j == 0), stop=(j == len(kts) - 1),
                           deps=[ex] + ([bankg.get(5)] if j == 0 else []))
                avmm = p.op("tensor", "matmul", cx.pb[6][0:64, :],
                            av[bi][:, kt * 64:(kt + 1) * 64], pt[:],
                            start=(j == 0), stop=(j == len(kts) - 1),
                            deps=[lv] + ([bankg.get(6)] if j == 0 else []))

            if dbg_r is not None:
                dcp = p.op("vector", "tensor_copy", cx.att_o[64:128, 0:512],
                           cx.pb[5][64:128, :], deps=[rmm, dbgg[0]])
                dbgg[0] = p.dma("sync",
                                dbg_r.ap()[idx, :, qb * 512:(qb + 1) * 512][0:64],
                                cx.att_o[64:128, 0:512], deps=[dcp])
            if dbg_av is not None:
                dcp2 = p.op("vector", "tensor_copy", cx.att_o[64:128, 0:512],
                            cx.pb[6][0:64, :], deps=[avmm, dbgg[0]])
                dbgg[0] = p.dma("sync",
                                dbg_av.ap()[idx, :, qb * 512:(qb + 1) * 512][0:64],
                                cx.att_o[64:128, 0:512], deps=[dcp2])
            ri = p.op("vector", "reciprocal", rin[0:64, 0:512],
                      cx.pb[5][0:64, :], deps=[rmm])
            bankg[5] = ri
            oc = p.op("vector", "tensor_tensor", cx.att_o[0:64, 0:512],
                      cx.pb[6][0:64, :], rin[0:64, 0:512], OP.mult,
                      deps=[ri, avmm, bufg[1 - bi].get("oc"), g.get("oc")])
            bankg[6] = oc
            st = p.dma("sync", out_hbm.ap()[out_off + ch, r0:r0 + 64,
                                            soff + qb0:soff + qb0 + 512],
                       cx.att_o[0:64, 0:512], deps=[oc])
            g["oc"] = st
        bufg[bi] = {"lastmm": last_st, "lastv": avmm, "oc": g.get("oc")}
    p.fence()



def rope_perm_halfsplit(nheads=16, hd=64):
    return np.arange(nheads * hd)


def rope_perm_interleaved(nheads=16, hd=64):
    perm = []
    for h in range(nheads):
        perm += [h * hd + 2 * j for j in range(hd // 2)]
        perm += [h * hd + 2 * j + 1 for j in range(hd // 2)]
    return np.array(perm)


def rope_tables(theta=10000.0, nfreq=32, seq=SEQ, swap_block=32):
    j = np.arange(nfreq, dtype=np.float32)
    inv = (1.0 / theta ** (j / nfreq)).astype(np.float32)
    ang = np.arange(seq, dtype=np.float32)[None, :] * inv[:, None]
    cosr = np.cos(ang).astype(np.float32)
    sinr = np.sin(ang).astype(np.float32)
    reps_half = swap_block // nfreq
    cos = np.tile(cosr, (128 // nfreq, 1))
    half = np.concatenate([np.tile(-sinr, (reps_half, 1)),
                           np.tile(sinr, (reps_half, 1))], 0)
    sin = np.tile(half, (64 // swap_block, 1))
    return cos.astype(BF), sin.astype(BF)


def bias_cols(b, perm=None):
    b = np.asarray(b, np.float32)
    if perm is not None:
        b = b[perm]
    return np.ascontiguousarray(b.reshape(-1, 128).T)


def din(cx, name, arr):
    if name in cx.d:
        return cx.d[name]
    dt = BF16 if arr.dtype == BF else F32
    h = cx.p.dram(name, list(arr.shape), dt, kind="ExternalInput")
    cx.d[name] = h
    return h


def load_tile(cx, dst, ap):
    tok = cx.p.dma("sync", dst, ap)
    cx.p.wait("sync", tok)
    cx.p.barrier()


def emit_bcast_row(cx, row_dram_ap, n, dst):
    p = cx.p
    ld = p.dma("sync", cx.stage2[0:1, 0:n], row_dram_ap)
    ck = p.op("vector", "tensor_copy", cx.bbA[0:1, 0:n], cx.stage2[0:1, 0:n],
              deps=[ld])
    toks = []
    for nb in range((n + 511) // 512):
        w0, w1 = nb * 512, min(n, nb * 512 + 512)
        mm = p.op("tensor", "matmul", cx.pb[nb][:, 0:w1 - w0],
                  cx.ones[0:1, :], cx.bbA[0:1, w0:w1], start=True, stop=True,
                  deps=[ck])
        toks.append(p.op("vector", "tensor_copy", dst[:, w0:w1],
                         cx.pb[nb][:, 0:w1 - w0], deps=[mm]))
    p.wait("vector", *toks)
    p.barrier()


def prep_l3(g):
    out = {}
    perm = rope_perm_interleaved()
    out["l3_wq"] = _fm(g["L3_attn_Wq"].T[:, perm])
    out["l3_wk"] = _fm(g["L3_attn_Wk"].T[:, perm])
    out["l3_wv"] = _fm(g["L3_attn_Wv"].T)
    out["l3_wo"] = _fm(g["L3_attn_Wo"].T)
    out["l3_bq"] = bias_cols(g["L3_attn_bq"], perm)
    out["l3_bk"] = bias_cols(g["L3_attn_bk"], perm)
    out["l3_bo"] = bias_cols(g["L3_attn_bo"])
    out["l3_bv"] = np.asarray(g["L3_attn_bv"], np.float32).reshape(1, 1024)
    cos, sin = rope_tables()
    out["l3_cos"] = cos
    out["l3_sin"] = sin
    return out


def emit_l3_attn(cx, prep, heads=None, skip_out=False):
    p = cx.p
    d = {k: din(cx, k, v) for k, v in prep.items() if k.startswith("l3_")}
    load_tile(cx, cx.cos[:], d["l3_cos"].ap())
    load_tile(cx, cx.sin[:], d["l3_sin"].ap())
    load_tile(cx, cx.bcols[:, 0:8], d["l3_bq"].ap())
    load_tile(cx, cx.bcols[:, 8:16], d["l3_bk"].ap())
    load_tile(cx, cx.bcols[:, 16:24], d["l3_bo"].ap())
    emit_bcast_row(cx, d["l3_bv"].ap(), 1024, cx.brow)
    emit_proj_fm2(cx, d["l3_wq"], cx.xn[:], 1024, 1024,
                  make_rope_evac(cx, cx.scrA, cx.cos, cx.sin,
                                 bias_sb=cx.bcols[:, 0:8], dram_off=0))
    emit_proj_fm2(cx, d["l3_wk"], cx.xn[:], 1024, 1024,
                  make_rope_evac(cx, cx.scrA, cx.cos, cx.sin,
                                 bias_sb=cx.bcols[:, 8:16], dram_off=8))
    emit_proj_tm(cx, d["l3_wv"], cx.xn[:], 1024, 1024, cx.scrB,
                 bias_row=cx.brow)
    emit_attention(cx, cx.scrA, cx.scrA, cx.scrB, cx.scrC, nheads=16,
                   causal=False, scale=0.125, q_off=0, k_off=8, heads=heads)
    if not skip_out:
        emit_proj_hbm_act(cx, d["l3_wo"], cx.scrC, 1024, 1024,
                          make_bias_evac(cx, cx.s_hbm, bias_sb=cx.bcols[:, 16:24]))


def prep_l1_attn(g):
    out = {}
    arr = np.asarray(g["L1_attn_w_qkv"], np.float32).T
    out["l1_wq"] = _fm(arr[:, 0:1024])
    out["l1_wk"] = _fm(arr[:, 1024:2048])
    out["l1_wv"] = _fm(arr[:, 2048:3072])
    out["l1_wo"] = _fm(np.asarray(g["L1_attn_w_out"], np.float32).T)
    cos, sin = rope_tables()
    out["l1_cos"] = cos
    out["l1_sin"] = sin
    return out


def emit_l1_attn(cx, prep, heads=None):
    p = cx.p
    d = {k: din(cx, k, v) for k, v in prep.items() if k.startswith("l1_")}
    load_tile(cx, cx.cos[:], d["l1_cos"].ap())
    load_tile(cx, cx.sin[:], d["l1_sin"].ap())
    emit_proj_fm2(cx, d["l1_wq"], cx.xn[:], 1024, 1024,
                  make_rope_evac(cx, cx.scrA, cx.cos, cx.sin, dram_off=0))
    emit_proj_fm2(cx, d["l1_wk"], cx.xn[:], 1024, 1024,
                  make_rope_evac(cx, cx.scrA, cx.cos, cx.sin, dram_off=8))
    emit_proj_tm(cx, d["l1_wv"], cx.xn[:], 1024, 1024, cx.scrB)
    emit_attention(cx, cx.scrA, cx.scrA, cx.scrB, cx.scrC, nheads=16,
                   causal=False, scale=0.125, q_off=0, k_off=8, heads=heads)
    emit_proj_hbm_act(cx, d["l1_wo"], cx.scrC, 1024, 1024,
                      make_bias_evac(cx, cx.s_hbm))


def prep_l2_attn(g):
    out = {}
    out["l2_wq"] = _fm(np.asarray(g["L2_attn_W_q"], np.float32))
    out["l2_wdkv"] = _fm(np.asarray(g["L2_attn_W_dkv"], np.float32))
    out["l2_wuk"] = _fm(np.asarray(g["L2_attn_W_uk"], np.float32))
    out["l2_wuv"] = _fm(np.asarray(g["L2_attn_W_uv"], np.float32))
    out["l2_wo"] = _fm(np.asarray(g["L2_attn_W_o"], np.float32))
    out["l2_gamma"] = bias_cols(g["L2_attn_ln_gamma"])
    out["l2_beta"] = bias_cols(g["L2_attn_ln_beta"])
    cos, sin = rope_tables()
    out["l2_cos"] = cos
    out["l2_sin"] = sin
    return out


def emit_lat_ln(cx, src_hbm, dst_hbm, nchunk, gamma, beta, src_off=0, dst_off=0,
                eps=1e-5):
    p = cx.p
    nfeat = nchunk * 128
    xs = cx.bbA
    xq = cx.bbB
    mu, rs, tv = cx.f512[0], cx.f512[1], cx.f512[2]
    eps_col = constcol(cx, eps)
    prev = []
    for t in range(TT512):
        t0 = t * 512
        ldt = [p.dma("sync", xs[:, c * 512:(c + 1) * 512],
                     src_hbm.ap()[src_off + c, :, t0:t0 + 512], deps=prev)
               for c in range(nchunk)]
        sqs = [p.op("scalar", "activation", xq[:, c * 512:(c + 1) * 512],
                    xs[:, c * 512:(c + 1) * 512], AF.Square, deps=[ldt[c]])
               for c in range(nchunk)]
        ps_s, ps_q = cx.pb[0], cx.pb[1]
        mm_s = mm_q = None
        for c in range(nchunk):
            sl = slice(c * 512, (c + 1) * 512)
            mm_s = p.op("tensor", "matmul", ps_s[:], cx.ones[:], xs[:, sl],
                        start=(c == 0), stop=(c == nchunk - 1), deps=[ldt[c]])
            mm_q = p.op("tensor", "matmul", ps_q[:], cx.ones[:], xq[:, sl],
                        start=(c == 0), stop=(c == nchunk - 1), deps=[sqs[c]])
        t1 = p.op("vector", "tensor_scalar_mul", mu[:], ps_s[:], 1.0 / nfeat,
                  deps=[mm_s])
        t2 = p.op("vector", "tensor_tensor", tv[:], mu[:], mu[:], OP.mult,
                  deps=[t1])
        t3 = p.op("vector", "scalar_tensor_tensor", tv[:], ps_q[:], 1.0 / nfeat,
                  tv[:], OP.mult, OP.subtract, deps=[mm_q, t2])
        t4 = p.op("scalar", "activation", rs[:], tv[:], AF.Ln, bias=eps_col[:],
                  deps=[t3])
        t5 = p.op("scalar", "activation", rs[:], rs[:], AF.Exp, scale=-0.5,
                  deps=[t4])
        prev = []
        for c in range(nchunk):
            sl = slice(c * 512, (c + 1) * 512)
            eng = "vector" if c % 2 == 0 else "gpsimd"
            u1 = p.op(eng, "tensor_tensor", xq[:, sl], xs[:, sl], mu[:],
                      OP.subtract, deps=[t1, sqs[c], mm_s, mm_q])
            u2 = p.op(eng, "tensor_tensor", xq[:, sl], xq[:, sl], rs[:],
                      OP.mult, deps=[t5, u1])
            if beta is not None:
                u3 = p.op(eng, "tensor_scalar", xs[:, sl], xq[:, sl],
                          gamma[:, c:c + 1], beta[:, c:c + 1], OP.mult, OP.add,
                          deps=[u2])
            else:
                u3 = p.op(eng, "tensor_scalar_mul", xs[:, sl], xq[:, sl],
                          gamma[:, c:c + 1], deps=[u2])
            wb = p.dma("scalar", dst_hbm.ap()[dst_off + c, :, t0:t0 + 512],
                       xs[:, sl], deps=[u3])
            prev.append(wb)
    p.fence()


def emit_l2_attn(cx, prep, heads=None):
    p = cx.p
    d = {k: din(cx, k, v) for k, v in prep.items() if k.startswith("l2_")}
    load_tile(cx, cx.cos[:], d["l2_cos"].ap())
    load_tile(cx, cx.sin[:], d["l2_sin"].ap())
    load_tile(cx, cx.bcols[:, 0:2], d["l2_gamma"].ap())
    load_tile(cx, cx.bcols[:, 2:4], d["l2_beta"].ap())
    emit_proj_fm2(cx, d["l2_wq"], cx.xn[:], 1024, 1024,
                  make_rope_evac(cx, cx.scrA, cx.cos, cx.sin, dram_off=0))
    emit_proj_fm2(cx, d["l2_wdkv"], cx.xn[:], 1024, 256,
                  make_bias_evac(cx, cx.scrA, dram_off=16))
    emit_lat_ln(cx, cx.scrA, cx.scrA, 2, cx.bcols[:, 0:2], cx.bcols[:, 2:4],
                src_off=16, dst_off=18)
    emit_proj_hbm_act(cx, d["l2_wuk"], cx.scrA, 256, 1024,
                      make_rope_evac(cx, cx.scrA, cx.cos, cx.sin, dram_off=8),
                      act_off=18)
    emit_proj_tm(cx, d["l2_wuv"], None, 256, 1024, cx.scrB,
                 act_hbm=cx.scrA, act_off=18)
    emit_attention(cx, cx.scrA, cx.scrA, cx.scrB, cx.scrC, nheads=16,
                   causal=True, scale=0.125, q_off=0, k_off=8, heads=heads)
    emit_proj_hbm_act(cx, d["l2_wo"], cx.scrC, 1024, 1024,
                      make_bias_evac(cx, cx.s_hbm))


def emit_tm2fm(cx, src_tm, dst_fm, dst_off=0, post="copy", x1_hbm=None,
               x1_off=0):
    p = cx.p
    ident = cx.ident
    ldg = [None, None]
    evg = [None, None]
    bankg = {}
    for tt in range(TT128):
        bi = tt % 2
        tin = cx.bbB[:, bi * 2048:bi * 2048 + 1024]
        ld = p.dma("sync", tin, src_tm.ap()[tt, :, 0:1024], deps=[ldg[bi]])
        if x1_hbm is not None:
            x1t = cx.bbA[:, bi * 2048:bi * 2048 + 1024]
            lx = p.dma("sync", x1t.rearrange("p (c j) -> p c j", j=128),
                       x1_hbm.ap()[x1_off:x1_off + 8, :, tt * 128:(tt + 1) * 128]
                       .rearrange("c p j -> p c j"), deps=[ldg[bi]])
        out = cx.evout[:, bi * 1024:bi * 1024 + 1024]
        evs = []
        for c in range(8):
            tr = p.op("tensor", "transpose", cx.pbbf[c][:, 0:128],
                      tin[:, c * 128:(c + 1) * 128], ident[:],
                      deps=[ld, bankg.get(c)])
            osl = out[:, c * 128:(c + 1) * 128]
            if post == "copy":
                ev = p.op("vector", "tensor_copy", osl, cx.pbbf[c][:, 0:128],
                          deps=[tr, evg[bi]])
            elif post == "silu":
                ev = p.op("scalar", "activation", osl, cx.pbbf[c][:, 0:128],
                          AF.Silu, deps=[tr, evg[bi]])
            else:
                t1 = cx.stage2[:, (c % 4) * 512:(c % 4) * 512 + 128]
                a1 = p.op("vector", "tensor_tensor", t1, cx.pbbf[c][:, 0:128],
                          x1t[:, c * 128:(c + 1) * 128], OP.add,
                          deps=[tr, lx, evg[bi], bankg.get(("t1", c % 4))])
                ev = p.op("scalar", "activation", osl, t1, AF.Silu,
                          deps=[a1, evg[bi]])
                bankg[("t1", c % 4)] = ev
            bankg[c] = ev
            evs.append(ev)
        st = p.dma("scalar", dst_fm.ap()[dst_off:dst_off + 8, :,
                                         tt * 128:(tt + 1) * 128]
                   .rearrange("c p j -> p c j"),
                   out.rearrange("p (c j) -> p c j", j=128), deps=evs)
        ldg[bi] = st
        evg[bi] = st
    p.fence()


def emit_moe_bias_init(cx, E, bias_rows_sb, acc_tm, cwoff=0):
    p = cx.p
    tg = None
    cg = None
    for tt in range(TT128):
        tr = p.op("tensor", "transpose", cx.pbbf[4][0:E, 0:128],
                  cx.cw[:, tt * E + cwoff:(tt + 1) * E + cwoff], cx.ident[:],
                  deps=[tg])
        cp = p.op("vector", "tensor_copy", cx.att_o[0:E, 0:128],
                  cx.pbbf[4][0:E, 0:128], deps=[tr, cg])
        tg = cp
        mm1 = p.op("tensor", "matmul", cx.pb[5][:], cx.att_o[0:E, 0:128],
                   bias_rows_sb[0:E, 0:512], start=True, stop=True, deps=[cp])
        mm2 = p.op("tensor", "matmul", cx.pb[6][:], cx.att_o[0:E, 0:128],
                   bias_rows_sb[0:E, 512:1024], start=True, stop=True, deps=[cp])
        e1 = p.op("vector", "tensor_copy", cx.evout[:, 0:512], cx.pb[5][:],
                  deps=[mm1])
        e2 = p.op("vector", "tensor_copy", cx.evout[:, 512:1024], cx.pb[6][:],
                  deps=[mm2])
        st = p.dma("scalar", acc_tm.ap()[tt, :, 0:1024], cx.evout[:, 0:1024],
                   deps=[e1, e2])
        cg = st
    p.fence()


def emit_moe_zero_init(cx, acc_tm):
    p = cx.p
    z = p.op("vector", "memset", cx.evout[:, 0:1024], 0.0)
    p.barrier()
    for tt in range(TT128):
        p.dma("sync", acc_tm.ap()[tt, :, 0:1024], cx.evout[:, 0:1024])
    p.fence()


def emit_moe_down_combine(cx, wd_hbm, gated_hbm, acc_tm, e, E, KCg=8,
                          gated_off=0, cwoff=0, weight_imm=None):
    p = cx.p
    assert KCg * 1024 <= 16384
    for k in range(KCg):
        ld = p.dma("sync", cx.stage[:, 0:1024], wd_hbm.ap()[k], deps=[])
        ck = p.op("vector", "tensor_copy", cx.wres[:, k * 1024:(k + 1) * 1024],
                  cx.stage[:, 0:1024], deps=[ld])
        p.wait("sync", ck)
    p.barrier()
    stageb = cx.stage.ap().bitcast(BF16)
    ldg = [None, None]
    acg = [None, None]
    for tt in range(TT128):
        bi = tt % 2
        gt = stageb[:, bi * 4096:bi * 4096 + KCg * 128]
        ld = p.dma("sync", gt.rearrange("p (k j) -> p k j", j=128),
                   gated_hbm.ap()[gated_off:gated_off + KCg, :,
                                  tt * 128:(tt + 1) * 128]
                   .rearrange("k p j -> p k j"), deps=[ldg[bi]])
        at = cx.bbB[:, bi * 2048:bi * 2048 + 1024]
        la = p.dma("scalar", at, acc_tm.ap()[tt, :, 0:1024], deps=[acg[bi]])
        ps = (cx.pb[bi * 2], cx.pb[bi * 2 + 1])
        mm = [None, None]
        for k in range(KCg):
            for nb in range(2):
                dep = [ld]
                if k == 0:
                    dep.append(acg[bi])
                mm[nb] = p.op("tensor", "matmul", ps[nb][:],
                              gt[:, k * 128:(k + 1) * 128],
                              cx.wres[:, k * 1024 + nb * 512:
                                      k * 1024 + nb * 512 + 512],
                              start=(k == 0), stop=(k == KCg - 1), deps=dep)
        wsc = (weight_imm if weight_imm is not None else
               cx.cw[:, tt * E + cwoff + e: tt * E + cwoff + e + 1])
        u1 = p.op("vector", "scalar_tensor_tensor", at[:, 0:512], ps[0][:],
                  wsc, at[:, 0:512], OP.mult, OP.add, deps=[mm[0], la])
        u2 = p.op("vector", "scalar_tensor_tensor", at[:, 512:1024], ps[1][:],
                  wsc, at[:, 512:1024], OP.mult, OP.add, deps=[mm[1], la])
        st = p.dma("scalar", acc_tm.ap()[tt, :, 0:1024], at, deps=[u1, u2])
        ldg[bi] = mm[0]
        acg[bi] = st
    p.fence()


def prep_l2_ffn(g):
    out = {}
    out["l2f_w1"] = _fm(np.asarray(g["L2_ffn_W1"], np.float32))
    out["l2f_b1"] = bias_cols(g["L2_ffn_b1"])
    out["l2f_w2"] = _fm(np.asarray(g["L2_ffn_W2"], np.float32))
    out["l2f_b2"] = np.asarray(g["L2_ffn_b2"], np.float32).reshape(1, 8)
    we = np.asarray(g["L2_ffn_expert_W"], np.float32)
    out["l2f_we"] = np.ascontiguousarray(
        we.reshape(8, 8, 128, 1024)).reshape(64, 128, 1024)
    out["l2f_be"] = np.asarray(g["L2_ffn_expert_b"], np.float32)
    return out


def emit_l2_ffn(cx, prep):
    p = cx.p
    d = {k: din(cx, k, v) for k, v in prep.items() if k.startswith("l2f_")}
    load_tile(cx, cx.bcols[:, 0:4], d["l2f_b1"].ap())
    emit_proj_fm2(cx, d["l2f_w1"], cx.xn[:], 1024, 512,
                  make_bias_evac(cx, cx.scrA, bias_sb=cx.bcols[:, 0:4],
                                 act_fn=AF.Relu, dram_off=20))
    w2sb = cx.att_o
    ld = p.dma("sync", cx.stage[:, 0:32].rearrange("p (k e) -> p k e", e=8),
               d["l2f_w2"].ap().rearrange("k p e -> p k e"))
    ck = p.op("vector", "tensor_copy", w2sb[:, 0:32], cx.stage[:, 0:32],
              deps=[ld])
    p.wait("vector", ck)
    emit_bcast_row(cx, d["l2f_b2"].ap(), 8, cx.brow[:, 0:8])
    p.barrier()
    stageb = cx.stage.ap().bitcast(BF16)
    lg = cx.f512[0]
    v8 = cx.f512[1]
    ldg = [None, None]
    p6g = None
    for tt in range(TT128):
        bi = tt % 2
        ht = stageb[:, bi * 4096:bi * 4096 + 512]
        ldh = p.dma("sync", ht.rearrange("p (k j) -> p k j", j=128),
                    cx.scrA.ap()[20:24, :, tt * 128:(tt + 1) * 128]
                    .rearrange("k p j -> p k j"), deps=[ldg[bi]])
        mm = None
        for k in range(4):
            mm = p.op("tensor", "matmul", cx.pb[6][:, 0:8],
                      ht[:, k * 128:(k + 1) * 128], w2sb[:, k * 8:(k + 1) * 8],
                      start=(k == 0), stop=(k == 3),
                      deps=[ldh] + ([p6g] if k == 0 else []))
        l1 = p.op("vector", "tensor_tensor", lg[:, 0:8], cx.pb[6][:, 0:8],
                  cx.brow[:, 0:8], OP.add, deps=[mm, ldg[1 - bi]])
        p6g = l1
        m8 = p.op("vector", "max", v8[:, 0:8], lg[:, 0:8], deps=[l1])
        mk = p.op("vector", "tensor_scalar", cx.cw[:, tt * 8:(tt + 1) * 8],
                  lg[:, 0:8], v8[:, 0:1], None, OP.is_ge, deps=[m8])
        ldg[bi] = mk
    p.fence()
    besb = cx.att_k[0]
    ldb = p.dma("sync", cx.stage[0:8, 0:1024], d["l2f_be"].ap())
    ckb = p.op("vector", "tensor_copy", besb[0:8, 0:1024], cx.stage[0:8, 0:1024],
               deps=[ldb])
    p.wait("vector", ckb)
    p.barrier()
    emit_moe_bias_init(cx, 8, besb, cx.scrB)
    weh = d["l2f_we"]
    for e in range(8):
        for k in range(8):
            ld = p.dma("sync", cx.stage[:, 0:1024], weh.ap()[e * 8 + k], deps=[])
            ck = p.op("vector", "tensor_copy",
                      cx.wres[:, k * 1024:(k + 1) * 1024],
                      cx.stage[:, 0:1024], deps=[ld])
            p.wait("sync", ck)
        p.barrier()
        acg = [None, None]
        for tt in range(TT128):
            bi = tt % 2
            at = cx.bbB[:, bi * 2048:bi * 2048 + 1024]
            la = p.dma("scalar", at, cx.scrB.ap()[tt, :, 0:1024], deps=[acg[bi]])
            ps = (cx.pb[bi * 2], cx.pb[bi * 2 + 1])
            mm = [None, None]
            for k in range(8):
                for nb in range(2):
                    dep = []
                    if k == 0:
                        dep = [acg[bi]]
                    mm[nb] = p.op("tensor", "matmul", ps[nb][:],
                                  cx.xn[:, k * T + tt * 128: k * T + tt * 128 + 128],
                                  cx.wres[:, k * 1024 + nb * 512:
                                          k * 1024 + nb * 512 + 512],
                                  start=(k == 0), stop=(k == 7), deps=dep)
            u1 = p.op("vector", "scalar_tensor_tensor", at[:, 0:512], ps[0][:],
                      cx.cw[:, tt * 8 + e: tt * 8 + e + 1], at[:, 0:512],
                      OP.mult, OP.add, deps=[mm[0], la])
            u2 = p.op("vector", "scalar_tensor_tensor", at[:, 512:1024],
                      ps[1][:], cx.cw[:, tt * 8 + e: tt * 8 + e + 1],
                      at[:, 512:1024], OP.mult, OP.add, deps=[mm[1], la])
            st = p.dma("scalar", cx.scrB.ap()[tt, :, 0:1024], at, deps=[u1, u2])
            acg[bi] = st
        p.fence()
    emit_tm2fm(cx, cx.scrB, cx.g_hbm, post="silu")


def prep_l3_ffn(g):
    out = {}
    out["l3f_wg"] = _fm(np.asarray(g["L3_ffn_Wg"], np.float32).T)
    out["l3f_bg"] = np.asarray(g["L3_ffn_bg"], np.float32).reshape(1, 16)
    wgu = np.asarray(g["L3_ffn_Wgu"], np.float32)
    out["l3f_wgate"] = np.ascontiguousarray(
        wgu[:, :, 0::2].reshape(16, 8, 128, 1024)).reshape(128, 128, 1024)
    out["l3f_wup"] = np.ascontiguousarray(
        wgu[:, :, 1::2].reshape(16, 8, 128, 1024)).reshape(128, 128, 1024)
    bgu = np.asarray(g["L3_ffn_bgu"], np.float32)
    out["l3f_bgate"] = np.ascontiguousarray(
        np.stack([bias_cols(bgu[e, 0::2]) for e in range(16)]))
    out["l3f_bup"] = np.ascontiguousarray(
        np.stack([bias_cols(bgu[e, 1::2]) for e in range(16)]))
    wd = np.asarray(g["L3_ffn_Wd"], np.float32)
    out["l3f_wd"] = np.ascontiguousarray(
        wd.reshape(16, 8, 128, 1024)).reshape(128, 128, 1024)
    out["l3f_bd"] = np.asarray(g["L3_ffn_bd"], np.float32)
    return out


def emit_glu_expert_gateup(cx, wg_hbm, wu_hbm, bg_sb, bu_sb, dst_hbm,
                           dst_off=0):
    p = cx.p
    stageb = cx.stage.ap().bitcast(BF16)
    slotg = {}
    for o in range(8):
        lds = []
        for wi, wh in ((0, wg_hbm), (1, wu_hbm)):
            ld = p.dma("sync", cx.wfs[wi][:, 0:1024]
                       .rearrange("p (k c) -> p k c", c=128),
                       wh.ap()[:, :, o * 128:(o + 1) * 128]
                       .rearrange("k p c -> p k c"),
                       deps=[slotg.get("wmm")])
            ck = p.op("vector", "tensor_copy", cx.wb[wi][:, 0:1024],
                      cx.wfs[wi][:, 0:1024], deps=[ld])
            lds.append(ck)
        for th in range(2):
            mmg = [None] * 4
            mmu = [None] * 4
            for k in range(8):
                for t4 in range(4):
                    t = th * 4 + t4
                    dep = list(lds) if k == 0 else []
                    if k == 0:
                        dep.append(slotg.get(("bank", t4)))
                    mmg[t4] = p.op("tensor", "matmul", cx.pb[t4][:],
                                   cx.wb[0][:, k * 128:(k + 1) * 128],
                                   cx.xn[:, k * T + t * 512: k * T + (t + 1) * 512],
                                   start=(k == 0), stop=(k == 7), deps=dep)
            for k in range(8):
                for t4 in range(4):
                    t = th * 4 + t4
                    dep = list(lds) if k == 0 else []
                    if k == 0:
                        dep.append(slotg.get(("bank", 4 + t4)))
                    mmu[t4] = p.op("tensor", "matmul", cx.pb[4 + t4][:],
                                   cx.wb[1][:, k * 128:(k + 1) * 128],
                                   cx.xn[:, k * T + t * 512: k * T + (t + 1) * 512],
                                   start=(k == 0), stop=(k == 7), deps=dep)
            slotg["wmm"] = mmu[3]
            for t4 in range(4):
                t = th * 4 + t4
                yc = stageb[:, t4 * 512:t4 * 512 + 512]
                yc2 = stageb[:, 2048 + t4 * 512:2048 + t4 * 512 + 512]
                sg = stageb[:, 4096 + t4 * 512:4096 + t4 * 512 + 512]
                glu = stageb[:, 6144 + t4 * 512:6144 + t4 * 512 + 512]
                yu = cx.bbA[:, t4 * 512:t4 * 512 + 512]
                yu2 = cx.bbB[:, t4 * 512:t4 * 512 + 512]
                out = cx.evout[:, t4 * 512:t4 * 512 + 512]
                g1 = p.op("scalar", "activation", yc, cx.pb[t4][:], AF.Identity,
                          bias=bg_sb[:, o:o + 1],
                          deps=[mmg[t4], slotg.get(("yc", t4))])
                g2 = p.op("vector", "tensor_scalar_min", yc2, yc, 7.0,
                          deps=[g1, slotg.get(("yc2", t4))])
                g3 = p.op("scalar", "activation", sg, yc2, AF.Sigmoid,
                          scale=1.702, deps=[g2, slotg.get(("sg", t4))])
                g4 = p.op("gpsimd", "tensor_tensor", glu, yc2, sg, OP.mult,
                          deps=[g3, slotg.get(("glu", t4))])
                u1 = p.op("scalar", "activation", yu, cx.pb[4 + t4][:],
                          AF.Identity, bias=bu_sb[:, o:o + 1],
                          deps=[mmu[t4], slotg.get(("yu", t4))])
                u2 = p.op("vector", "tensor_scalar", yu2, yu, 7.0, -7.0,
                          OP.min, OP.max, deps=[u1, slotg.get(("yu2", t4))])
                gt = p.op("vector", "scalar_tensor_tensor", out, yu2, 1.0, glu,
                          OP.add, OP.mult, deps=[u2, g4, slotg.get(("out", t4))])
                st = p.dma("sync", dst_hbm.ap()[dst_off + o, :,
                                                t * 512:(t + 1) * 512],
                           out, deps=[gt])
                slotg[("bank", t4)] = g1
                slotg[("bank", 4 + t4)] = u1
                slotg[("yc", t4)] = g2
                slotg[("yc2", t4)] = g4
                slotg[("sg", t4)] = g4
                slotg[("glu", t4)] = gt
                slotg[("yu", t4)] = u2
                slotg[("yu2", t4)] = gt
                slotg[("out", t4)] = st
    p.fence()


def emit_l3_ffn(cx, prep):
    p = cx.p
    d = {k: din(cx, k, v) for k, v in prep.items() if k.startswith("l3f_")}
    wgsb = cx.att_o
    ld = p.dma("sync", cx.stage[:, 0:128].rearrange("p (k e) -> p k e", e=16),
               d["l3f_wg"].ap().rearrange("k p e -> p k e"))
    ck = p.op("vector", "tensor_copy", wgsb[:, 0:128], cx.stage[:, 0:128],
              deps=[ld])
    p.wait("vector", ck)
    emit_bcast_row(cx, d["l3f_bg"].ap(), 16, cx.brow[:, 0:16])
    p.barrier()
    lg, v8, sc1, sc2 = cx.f512[0], cx.f512[1], cx.f512[2], cx.f512[3]
    ldg = [None, None]
    p6g = None
    for tt in range(TT128):
        bi = tt % 2
        mm = None
        for k in range(8):
            mm = p.op("tensor", "matmul", cx.pb[6][:, 0:16],
                      cx.xn[:, k * T + tt * 128: k * T + tt * 128 + 128],
                      wgsb[:, k * 16:(k + 1) * 16],
                      start=(k == 0), stop=(k == 7),
                      deps=[p6g] if k == 0 else [])
        l1 = p.op("vector", "tensor_tensor", lg[:, 0:16], cx.pb[6][:, 0:16],
                  cx.brow[:, 0:16], OP.add, deps=[mm, ldg[1 - bi]])
        p6g = l1
        ldg[bi] = l1
        m8 = p.op("vector", "max", v8[:, 0:8], lg[:, 0:16], deps=[l1])
        nm1 = p.op("vector", "tensor_scalar_mul", sc1[:, 0:1], v8[:, 0:1], -1.0,
                   deps=[m8, ldg[1 - bi]])
        ee = p.op("scalar", "activation", sc2[:, 0:16], lg[:, 0:16], AF.Exp,
                  bias=sc1[:, 0:1], deps=[nm1, l1])
        dd = p.op("vector", "tensor_tensor", sc1[:, 1:2], v8[:, 1:2],
                  sc1[:, 0:1], OP.add, deps=[nm1, m8])
        de = p.op("scalar", "activation", sc1[:, 2:3], sc1[:, 1:2], AF.Exp,
                  deps=[dd])
        dn = p.op("vector", "tensor_scalar_add", sc1[:, 3:4], sc1[:, 2:3], 1.0,
                  deps=[de])
        rc = p.op("vector", "reciprocal", sc1[:, 4:5], sc1[:, 3:4], deps=[dn])
        msk = p.op("vector", "tensor_scalar", sc2[:, 16:32], lg[:, 0:16],
                   v8[:, 1:2], None, OP.is_ge, deps=[m8, ee])
        w1 = p.op("vector", "tensor_scalar", sc2[:, 0:16], sc2[:, 0:16],
                  sc1[:, 4:5], None, OP.mult, deps=[rc, ee])
        w2 = p.op("vector", "tensor_tensor", cx.cw[:, tt * 16:(tt + 1) * 16],
                  sc2[:, 0:16], sc2[:, 16:32], OP.mult, deps=[w1, msk])
        ldg[bi] = w2
    p.fence()
    bdsb = cx.att_k[0]
    ldb = p.dma("sync", cx.stage[0:16, 0:1024], d["l3f_bd"].ap())
    ckb = p.op("vector", "tensor_copy", bdsb[0:16, 0:1024],
               cx.stage[0:16, 0:1024], deps=[ldb])
    p.wait("vector", ckb)
    p.barrier()
    emit_moe_bias_init(cx, 16, bdsb, cx.scrB)
    wgate = d["l3f_wgate"]
    wup = d["l3f_wup"]
    wd = d["l3f_wd"]
    bgate = din(cx, "l3f_bgate", prep["l3f_bgate"])
    bup = din(cx, "l3f_bup", prep["l3f_bup"])
    for e in range(16):
        lb1 = p.dma("sync", cx.bcols[:, 8:16], bgate.ap()[e])
        lb2 = p.dma("sync", cx.bcols[:, 16:24], bup.ap()[e])
        p.wait("sync", lb1, lb2)
        p.barrier()
        wg_e = _DramView(wgate, e * 8)
        wu_e = _DramView(wup, e * 8)
        emit_glu_expert_gateup(cx, wg_e, wu_e, cx.bcols[:, 8:16],
                               cx.bcols[:, 16:24], cx.scrA, dst_off=24 - 24)
        emit_moe_down_combine(cx, _DramView(wd, e * 8), cx.scrA, cx.scrB,
                              e, 16, KCg=8, gated_off=0)
    emit_tm2fm(cx, cx.scrB, cx.g_hbm, post="silu")


class _DramView:
    def __init__(self, h, off, n=8):
        self.h = h
        self.off = off
        self.n = n

    def ap(self):
        return self.h.ap()[self.off:self.off + self.n]


def _rope_colperm_rd32():
    perm = np.zeros(512, np.int64)
    for h in range(16):
        c, hh = h // 4, h % 4
        for j in range(16):
            perm[c * 128 + hh * 16 + j] = h * 32 + j
            perm[c * 128 + 64 + hh * 16 + j] = h * 32 + 16 + j
    inv = np.argsort(perm)
    return perm


def prep_l1_ffn(g):
    out = {}
    out["f1_kvd"] = _fm(np.asarray(g["L1_ffn_kv_d"], np.float32).T)
    out["f1_qd"] = _fm(np.asarray(g["L1_ffn_q_d"], np.float32).T)
    qu = np.asarray(g["L1_ffn_q_u"], np.float32).T.copy()
    ku = np.asarray(g["L1_ffn_k_u"], np.float32).T.copy()
    for h in range(16):
        qu[:, h * 64:h * 64 + 32] = 0.0
        ku[:, h * 64:h * 64 + 32] = 0.0
    out["f1_qu"] = _fm(qu)
    out["f1_ku"] = _fm(ku)
    perm = _rope_colperm_rd32()
    out["f1_ropeq"] = _fm(np.asarray(g["L1_ffn_rope_q_w"], np.float32).T[:, perm])
    out["f1_ropek"] = _fm(np.asarray(g["L1_ffn_rope_k_w"], np.float32).T[:, perm])
    out["f1_vu"] = _fm(np.asarray(g["L1_ffn_v_u"], np.float32).T)
    out["f1_ow"] = _fm(np.asarray(g["L1_ffn_o_w"], np.float32).T)
    out["f1_ln1"] = bias_cols(g["L1_ffn_ln1_w"])
    out["f1_ln2"] = bias_cols(g["L1_ffn_ln2_w"])
    out["f1_router"] = np.ascontiguousarray(
        np.asarray(g["L1_ffn_router_w"], np.float32).T.reshape(8, 128, 7))
    out["f1_rbias"] = np.asarray(g["L1_ffn_routing_bias"], np.float32).reshape(1, 7)
    out["f1_sg"] = _fm(np.asarray(g["L1_ffn_shared_gate"], np.float32)[0].T)
    out["f1_su"] = _fm(np.asarray(g["L1_ffn_shared_up"], np.float32)[0].T)
    out["f1_sd"] = _fm(np.asarray(g["L1_ffn_shared_down"], np.float32)[0].T)
    rg = np.asarray(g["L1_ffn_routed_gate"], np.float32)
    ru = np.asarray(g["L1_ffn_routed_up"], np.float32)
    rd = np.asarray(g["L1_ffn_routed_down"], np.float32)
    out["f1_rg"] = np.ascontiguousarray(
        np.stack([_fm(rg[e].T) for e in range(7)])).reshape(56, 128, 2048)
    out["f1_ru"] = np.ascontiguousarray(
        np.stack([_fm(ru[e].T) for e in range(7)])).reshape(56, 128, 2048)
    out["f1_rd"] = np.ascontiguousarray(
        np.stack([_fm(rd[e].T) for e in range(7)])).reshape(112, 128, 1024)
    cos, sin = rope_tables(nfreq=16, swap_block=64)
    out["f1_cos"] = cos
    out["f1_sin"] = sin
    return out


def emit_ln_sb(cx, src_sb, dst_hbm, dst_off, w_cols=None, eps=1e-5):
    p = cx.p
    xq = cx.bbB
    mu, rs, tv = cx.f512[0], cx.f512[1], cx.f512[2]
    eps_col = constcol(cx, eps)
    prev = []
    for t in range(TT512):
        t0 = t * 512
        sqs = [p.op("scalar", "activation", xq[:, c * 512:(c + 1) * 512],
                    src_sb[:, c * T + t0: c * T + t0 + 512], AF.Square,
                    deps=prev)
               for c in range(NC)]
        ps_s, ps_q = cx.pb[0], cx.pb[1]
        mm_s = mm_q = None
        for c in range(NC):
            mm_s = p.op("tensor", "matmul", ps_s[:], cx.ones[:],
                        src_sb[:, c * T + t0: c * T + t0 + 512],
                        start=(c == 0), stop=(c == NC - 1),
                        deps=[prev[0]] if (c == 0 and prev) else [])
            mm_q = p.op("tensor", "matmul", ps_q[:], cx.ones[:],
                        xq[:, c * 512:(c + 1) * 512],
                        start=(c == 0), stop=(c == NC - 1), deps=[sqs[c]])
        t1 = p.op("vector", "tensor_scalar_mul", mu[:], ps_s[:], 1.0 / D,
                  deps=[mm_s] + prev)
        t2 = p.op("vector", "tensor_tensor", tv[:], mu[:], mu[:], OP.mult,
                  deps=[t1])
        t3 = p.op("vector", "scalar_tensor_tensor", tv[:], ps_q[:], 1.0 / D,
                  tv[:], OP.mult, OP.subtract, deps=[mm_q, t2])
        t4 = p.op("scalar", "activation", rs[:], tv[:], AF.Ln, bias=eps_col[:],
                  deps=[t3])
        t5 = p.op("scalar", "activation", rs[:], rs[:], AF.Exp, scale=-0.5,
                  deps=[t4])
        prev = []
        for c in range(NC):
            sl = slice(c * 512, (c + 1) * 512)
            eng = "vector" if c % 2 == 0 else "gpsimd"
            u1 = p.op(eng, "tensor_tensor", xq[:, sl],
                      src_sb[:, c * T + t0: c * T + t0 + 512], mu[:],
                      OP.subtract, deps=[t1, sqs[c], mm_s, mm_q])
            u2 = p.op(eng, "tensor_tensor", xq[:, sl], xq[:, sl], rs[:],
                      OP.mult, deps=[t5, u1])
            if w_cols is not None:
                u2 = p.op(eng, "tensor_scalar_mul", xq[:, sl], xq[:, sl],
                          w_cols[:, c:c + 1], deps=[u2])
            wb = p.dma("scalar", dst_hbm.ap()[dst_off + c, :, t0:t0 + 512],
                       xq[:, sl], deps=[u2])
            prev.append(wb)
    p.fence()


def emit_rope_assemble(cx, rot_hbm, rot_off, eff_hbm, eff_off):
    p = cx.p
    toks = []
    for c in range(4):
        rv = rot_hbm.ap()[rot_off + c]
        for hh2 in range(2):
            ev = eff_hbm.ap()[eff_off + 2 * c + hh2]
            ev4 = ev.rearrange("(hhp r2) t -> hhp r2 t", hhp=2)
            src_a = rv.rearrange("(half hh2 hhp r) t -> half hh2 hhp r t",
                                 half=2, hh2=2, hhp=2)[0, hh2]
            toks.append(p.dma("sync", ev4[:, 0:16, :], src_a))
            src_b = rv.rearrange("(half hh2 hhp r) t -> half hh2 hhp r t",
                                 half=2, hh2=2, hhp=2)[1, hh2]
            toks.append(p.dma("sync", ev4[:, 16:32, :], src_b))
    p.wait("sync", *toks)
    p.fence()


def make_addxn_evac(cx, out_hbm, dram_off=0):
    p = cx.p
    outg = [None] * 8

    def evac(o, t, mtok, bank):
        i8 = (o + t) % 8
        out = cx.evout[:, i8 * 512:i8 * 512 + 512]
        ev = p.op("vector", "tensor_tensor", out, cx.pb[bank][:],
                  cx.xn[:, o * T + t * 512: o * T + (t + 1) * 512], OP.add,
                  deps=[mtok, outg[i8]])
        st = p.dma("sync", out_hbm.ap()[dram_off + o, :, t * 512:(t + 1) * 512],
                   out, deps=[ev])
        outg[i8] = st
        return ev

    return evac


def emit_silu_expert_gateup(cx, wg_hbm, wu_hbm, dst_hbm, dst_off=0, NCo=16,
                            wg_off=0, wu_off=0):
    p = cx.p
    stageb = cx.stage.ap().bitcast(BF16)
    slotg = {}
    for o in range(NCo):
        lds = []
        for wi, wh, woff in ((0, wg_hbm, wg_off), (1, wu_hbm, wu_off)):
            ld = p.dma("sync", cx.wfs[wi][:, 0:1024]
                       .rearrange("p (k c) -> p k c", c=128),
                       wh.ap()[woff:woff + 8, :, o * 128:(o + 1) * 128]
                       .rearrange("k p c -> p k c"),
                       deps=[slotg.get("wmm")])
            ck = p.op("vector", "tensor_copy", cx.wb[wi][:, 0:1024],
                      cx.wfs[wi][:, 0:1024], deps=[ld])
            lds.append(ck)
        for th in range(2):
            mmg = [None] * 4
            mmu = [None] * 4
            for k in range(8):
                for t4 in range(4):
                    t = th * 4 + t4
                    dep = list(lds) if k == 0 else []
                    if k == 0:
                        dep.append(slotg.get(("bank", t4)))
                    mmg[t4] = p.op("tensor", "matmul", cx.pb[t4][:],
                                   cx.wb[0][:, k * 128:(k + 1) * 128],
                                   cx.xn[:, k * T + t * 512: k * T + (t + 1) * 512],
                                   start=(k == 0), stop=(k == 7), deps=dep)
            for k in range(8):
                for t4 in range(4):
                    t = th * 4 + t4
                    dep = list(lds) if k == 0 else []
                    if k == 0:
                        dep.append(slotg.get(("bank", 4 + t4)))
                    mmu[t4] = p.op("tensor", "matmul", cx.pb[4 + t4][:],
                                   cx.wb[1][:, k * 128:(k + 1) * 128],
                                   cx.xn[:, k * T + t * 512: k * T + (t + 1) * 512],
                                   start=(k == 0), stop=(k == 7), deps=dep)
            slotg["wmm"] = mmu[3]
            for t4 in range(4):
                t = th * 4 + t4
                glu = stageb[:, t4 * 512:t4 * 512 + 512]
                out = cx.evout[:, t4 * 512:t4 * 512 + 512]
                g1 = p.op("scalar", "activation", glu, cx.pb[t4][:], AF.Silu,
                          deps=[mmg[t4], slotg.get(("glu", t4))])
                gt = p.op("vector", "tensor_tensor", out, cx.pb[4 + t4][:],
                          glu, OP.mult,
                          deps=[g1, mmu[t4], slotg.get(("out", t4))])
                st = p.dma("sync", dst_hbm.ap()[dst_off + o, :,
                                                t * 512:(t + 1) * 512],
                           out, deps=[gt])
                slotg[("bank", t4)] = g1
                slotg[("bank", 4 + t4)] = gt
                slotg[("glu", t4)] = gt
                slotg[("out", t4)] = st
    p.fence()


def emit_l1_ffn(cx, prep, heads=None):
    p = cx.p
    d = {k: din(cx, k, v) for k, v in prep.items() if k.startswith("f1_")}
    load_tile(cx, cx.bcols[:, 8:16], d["f1_ln1"].ap())
    emit_ln_sb(cx, cx.xn, cx.scrA, 16, w_cols=cx.bcols[:, 8:16])
    emit_proj_hbm_act(cx, d["f1_kvd"], cx.scrA, 1024, 256,
                      make_bias_evac(cx, cx.scrA, dram_off=24), act_off=16)
    emit_proj_hbm_act(cx, d["f1_qd"], cx.scrA, 1024, 256,
                      make_bias_evac(cx, cx.scrA, dram_off=26), act_off=16)
    emit_proj_hbm_act(cx, d["f1_qu"], cx.scrA, 256, 1024,
                      make_bias_evac(cx, cx.scrA, dram_off=0), act_off=26)
    emit_proj_hbm_act(cx, d["f1_ku"], cx.scrA, 256, 1024,
                      make_bias_evac(cx, cx.scrA, dram_off=8), act_off=24)
    load_tile(cx, cx.cos[:], d["f1_cos"].ap())
    load_tile(cx, cx.sin[:], d["f1_sin"].ap())
    emit_proj_hbm_act(cx, d["f1_ropeq"], cx.scrA, 256, 512,
                      make_rope_evac(cx, cx.scrC, cx.cos, cx.sin, dram_off=0,
                                     swap_block=64), act_off=26)
    emit_proj_hbm_act(cx, d["f1_ropek"], cx.scrA, 1024, 512,
                      make_rope_evac(cx, cx.scrC, cx.cos, cx.sin, dram_off=4,
                                     swap_block=64), act_off=16)
    emit_rope_assemble(cx, cx.scrC, 0, cx.scrA, 0)
    emit_rope_assemble(cx, cx.scrC, 4, cx.scrA, 8)
    emit_proj_tm(cx, d["f1_vu"], None, 256, 1024, cx.scrB,
                 act_hbm=cx.scrA, act_off=24)
    emit_attention(cx, cx.scrA, cx.scrA, cx.scrB, cx.scrC, nheads=16,
                   causal=True, scale=0.125, q_off=0, k_off=8, heads=heads)
    emit_proj_hbm_act(cx, d["f1_ow"], cx.scrC, 1024, 1024,
                      make_addxn_evac(cx, cx.scrA, dram_off=16))
    load_tile(cx, cx.bcols[:, 16:24], d["f1_ln2"].ap())
    emit_lat_ln(cx, cx.scrA, cx.scrA, 8, cx.bcols[:, 16:24],
                None, src_off=16, dst_off=24)
    tk = [p.dma("sync", cx.xn[:, c * T:(c + 1) * T], cx.scrA.ap()[24 + c])
          for c in range(8)]
    p.wait("sync", *tk)
    p.fence()
    wrsb = cx.att_o
    ld = p.dma("sync", cx.stage[:, 0:56].rearrange("p (k e) -> p k e", e=7),
               d["f1_router"].ap().rearrange("k p e -> p k e"))
    ck = p.op("vector", "tensor_copy", wrsb[:, 0:56], cx.stage[:, 0:56],
              deps=[ld])
    p.wait("vector", ck)
    emit_bcast_row(cx, d["f1_rbias"].ap(), 7, cx.brow[:, 0:8])
    p.barrier()
    stageb = cx.stage.ap().bitcast(BF16)
    lg, v8, sc1 = cx.f512[0], cx.f512[1], cx.f512[2]
    ldg = [None, None]
    p6g = None
    for tt in range(TT128):
        bi = tt % 2
        xt = stageb[:, bi * 4096:bi * 4096 + 1024]
        ldh = p.dma("sync", xt.rearrange("p (k j) -> p k j", j=128),
                    cx.scrA.ap()[24:32, :, tt * 128:(tt + 1) * 128]
                    .rearrange("k p j -> p k j"), deps=[ldg[bi]])
        mm = None
        for k in range(8):
            mm = p.op("tensor", "matmul", cx.pb[6][:, 0:7],
                      xt[:, k * 128:(k + 1) * 128], wrsb[:, k * 7:(k + 1) * 7],
                      start=(k == 0), stop=(k == 7),
                      deps=[ldh] + ([p6g] if k == 0 else []))
        l1 = p.op("vector", "tensor_tensor", lg[:, 0:7], cx.pb[6][:, 0:7],
                  cx.brow[:, 0:7], OP.add, deps=[mm, ldg[1 - bi]])
        p6g = l1
        pr = p.op("scalar", "activation", lg[:, 0:7], lg[:, 0:7], AF.Sigmoid,
                  deps=[l1])
        z8 = p.op("vector", "memset", lg[:, 7:8], -1.0, deps=[l1])
        m8 = p.op("vector", "max", v8[:, 0:8], lg[:, 0:8], deps=[pr, z8])
        s12 = p.op("vector", "tensor_tensor", sc1[:, 0:1], v8[:, 0:1],
                   v8[:, 1:2], OP.add, deps=[m8, ldg[1 - bi]])
        rc = p.op("vector", "reciprocal", sc1[:, 1:2], sc1[:, 0:1], deps=[s12])
        msk = p.op("vector", "tensor_scalar", lg[:, 8:15], lg[:, 0:7],
                   v8[:, 1:2], None, OP.is_ge, deps=[m8])
        w1 = p.op("vector", "tensor_scalar", lg[:, 0:7], lg[:, 0:7],
                  sc1[:, 1:2], None, OP.mult, deps=[rc, msk])
        w2 = p.op("vector", "tensor_tensor", cx.cw[:, tt * 8:tt * 8 + 7],
                  lg[:, 0:7], lg[:, 8:15], OP.mult, deps=[w1])
        ldg[bi] = w2
    p.fence()
    emit_moe_zero_init(cx, cx.scrB)
    emit_silu_expert_gateup(cx, d["f1_sg"], d["f1_su"], cx.scrA, dst_off=0,
                            NCo=16)
    emit_moe_down_combine(cx, d["f1_sd"], cx.scrA, cx.scrB, 0, 8, KCg=16,
                          weight_imm=1.0)
    for e in range(7):
        emit_silu_expert_gateup(cx, _DramView(d["f1_rg"], e * 8, 8),
                                _DramView(d["f1_ru"], e * 8, 8),
                                cx.scrA, dst_off=0, NCo=16)
        emit_moe_down_combine(cx, _DramView(d["f1_rd"], e * 16, 16),
                              cx.scrA, cx.scrB, e, 8, KCg=16)
    emit_tm2fm(cx, cx.scrB, cx.g_hbm, post="addsilu", x1_hbm=cx.scrA, x1_off=16)


FP4_MID = [0.25, 0.75, 1.25, 1.75, 2.5, 3.5, 5.0]
FP4_STEP = [0.5, 0.5, 0.5, 0.5, 1.0, 1.0, 2.0]


def emit_fp4_quant(cx, dst, src, W, slots, deps=(), eng1="vector",
                   eng2="gpsimd"):
    p = cx.p
    nb = W // 16
    sc, rc = slots["sc"][:, 0:nb], slots["rc"][:, 0:nb]
    f1 = slots["f1"][:, 0:W]
    f2 = slots["f2"][:, 0:W]
    b1 = slots["b1"][:, 0:W]
    b2 = slots["b2"][:, 0:W]
    dl = list(deps)
    src3 = src.rearrange("p (nb b) -> p nb b", b=16)
    r1 = p.op(eng1, "tensor_reduce", sc, src3, axis=AX.X, op=OP.max,
              apply_absolute_value=True, deps=dl)
    s2 = p.op(eng1, "tensor_scalar", sc, sc, 1.0 / 6.0, 1e-12, OP.mult, OP.max,
              deps=[r1])
    rcx = p.op(eng1, "reciprocal", rc, sc, deps=[s2] + dl)
    scb = rc.unsqueeze(2).to_broadcast([128, nb, 16])
    v = p.op(eng1, "tensor_tensor", f1.rearrange("p (nb b) -> p nb b", b=16),
             src3, scb, OP.mult, deps=[rcx] + dl)
    av = p.op("scalar", "activation", f2, f1, AF.Abs, deps=[v] + dl)
    sg = p.op("scalar", "sign", b1, f1, deps=[v] + dl)
    q = p.op(eng1, "tensor_scalar", b2, f2, FP4_MID[0], FP4_STEP[0],
             OP.is_gt, OP.mult, deps=[av] + dl)
    prevw = [q, sg]
    for i in range(1, 7):
        eng = eng2 if i % 2 == 0 else eng1
        mi = p.op(eng, "tensor_scalar", f1, f2, FP4_MID[i], FP4_STEP[i],
                  OP.is_gt, OP.mult, deps=[av] + prevw)
        mi2 = p.op(eng, "tensor_tensor", b2, b2, f1, OP.add,
                   deps=[mi] + prevw)
        prevw = [mi2]
    qs = p.op(eng1, "tensor_tensor", b2, b2, b1, OP.mult, deps=prevw + [sg])
    scb2 = sc.unsqueeze(2).to_broadcast([128, nb, 16])
    dq = p.op(eng1, "tensor_tensor", dst.rearrange("p (nb b) -> p nb b", b=16),
              b2.rearrange("p (nb b) -> p nb b", b=16), scb2, OP.mult,
              deps=[qs, s2])
    return dq


def prep_l0(g):
    out = {}
    ipw = np.asarray(g["L0_attn_in_proj_w"], np.float32)
    out["l0_wgate"] = _fm(ipw[:, 0:2048])
    out["l0_whs"] = _fm(ipw[:, 2048:4352])
    wdt = np.zeros((1024, 128), np.float32)
    wdt[:, 0:32] = ipw[:, 4352:4384]
    out["l0_wdt"] = _fm(wdt)
    cw = np.asarray(g["L0_attn_conv_w"], np.float32).reshape(2304, 4)
    out["l0_convw"] = np.ascontiguousarray(
        cw.reshape(18, 128, 4).transpose(1, 2, 0).reshape(128, 72))
    out["l0_convb"] = np.ascontiguousarray(
        np.asarray(g["L0_attn_conv_b"], np.float32).reshape(18, 128).T)
    out["l0_dtbias"] = np.asarray(g["L0_attn_dt_bias"], np.float32).reshape(32, 1)
    out["l0_alog"] = np.asarray(g["L0_attn_A_log"], np.float32).reshape(32, 1)
    out["l0_drow"] = np.asarray(g["L0_attn_D"], np.float32).reshape(1, 32)
    out["l0_normw"] = np.asarray(g["L0_attn_norm_w"], np.float32).reshape(1, 2048)
    out["l0_outw"] = _fm(np.asarray(g["L0_attn_out_proj_w"], np.float32))
    jj = np.arange(256)[None, :]
    ii = np.arange(128)[:, None]
    out["l0_triu0"] = (jj >= ii).astype(np.float32).astype(BF)
    out["l0_triu1"] = (jj >= ii + 128).astype(np.float32).astype(BF)
    out["l0f_wg"] = np.ascontiguousarray(
        np.asarray(g["L0_ffn_w_gate"], np.float32).reshape(22, 128, 1024))
    out["l0f_wu"] = np.ascontiguousarray(
        np.asarray(g["L0_ffn_w_up"], np.float32).reshape(22, 128, 1024))
    out["l0f_wd"] = np.ascontiguousarray(
        np.asarray(g["L0_ffn_w_down"], np.float32).reshape(8, 128, 2816))
    return out


def emit_l0_quant_weights(cx, prep):
    p = cx.p
    d = {k: din(cx, k, v) for k, v in prep.items() if k.startswith("l0f_")}
    cx.wq_gate = p.dram("wq_gate", [8, 128, 2816], BF16)
    cx.wq_up = p.dram("wq_up", [8, 128, 2816], BF16)
    cx.wq_down = p.dram("wq_down", [22, 128, 1024], BF16)
    stageb = cx.stage.ap().bitcast(BF16)
    qsl = {"sc": cx.f512[0][:, 0:64], "rc": cx.f512[1][:, 0:64],
           "f1": cx.stage2[:, 0:1024], "f2": cx.stage2[:, 1024:2048],
           "b1": cx.bbA[:, 0:1024], "b2": cx.bbA[:, 1024:2048]}
    bankg = {}
    for (wh, dst, nc_, kw) in ((d["l0f_wg"], cx.wq_gate, 22, 1024),
                               (d["l0f_wu"], cx.wq_up, 22, 1024),
                               (d["l0f_wd"], cx.wq_down, 8, 2816)):
        for c in range(nc_):
            for p0 in range(0, kw, 1024):
                p1 = min(kw, p0 + 1024)
                pw = p1 - p0
                ld = p.dma("sync", cx.stage[:, 0:pw], wh.ap()[c, :, p0:p1])
                qs2 = dict(qsl)
                qs2["f1"] = cx.stage2[:, 0:1024]
                qs2["f2"] = cx.stage2[:, 1024:2048]
                qs2["b1"] = cx.bbA[:, 0:1024]
                qs2["b2"] = cx.bbA[:, 1024:2048]
                dq = emit_fp4_quant(cx, stageb[:, 4096:4096 + pw],
                                    cx.stage[:, 0:pw], pw, qs2, deps=[ld])
                for j in range(pw // 128):
                    jj = (p0 + j * 128) // 128
                    tr = p.op("tensor", "transpose", cx.pbbf[j % 4][:, 0:128],
                              stageb[:, 4096 + j * 128:4096 + (j + 1) * 128],
                              cx.ident[:], deps=[dq, bankg.get(("b", j % 4))])
                    cpt = p.op("vector", "tensor_copy",
                               cx.evout[:, (j % 8) * 128:(j % 8) * 128 + 128],
                               cx.pbbf[j % 4][:, 0:128],
                               deps=[tr, bankg.get(("e", j % 8))])
                    bankg[("b", j % 4)] = cpt
                    st = p.dma("sync", dst.ap()[jj, :, c * 128:(c + 1) * 128],
                               cx.evout[:, (j % 8) * 128:(j % 8) * 128 + 128],
                               deps=[cpt])
                    bankg[("e", j % 8)] = st
                p.fence()


def emit_l0_ffn(cx, prep):
    p = cx.p
    stageb = cx.stage.ap().bitcast(BF16)
    for tt in range(TT128):
        tm = cx.bbB[:, 0:1024]
        bkg = {}
        for c in range(NC):
            tr = p.op("tensor", "transpose", cx.pbbf[c % 4][:, 0:128],
                      cx.xn[:, c * T + tt * 128: c * T + tt * 128 + 128],
                      cx.ident[:], deps=[bkg.get(c % 4)])
            bkg[c % 4] = p.op("vector", "tensor_copy",
                              tm[:, c * 128:(c + 1) * 128],
                              cx.pbbf[c % 4][:, 0:128], deps=[tr])
        p.barrier()
        qsl = {"sc": cx.f512[0][:, 0:64], "rc": cx.f512[1][:, 0:64],
               "f1": cx.stage2[:, 0:1024], "f2": cx.stage2[:, 1024:2048],
               "b1": cx.bbA[:, 0:1024], "b2": cx.bbA[:, 1024:2048]}
        emit_fp4_quant(cx, tm, tm, 1024, qsl)
        p.barrier()
        bkg = {}
        for c in range(NC):
            tr = p.op("tensor", "transpose", cx.pbbf[c % 4][:, 0:128],
                      tm[:, c * 128:(c + 1) * 128], cx.ident[:],
                      deps=[bkg.get(c % 4)])
            bkg[c % 4] = p.op("vector", "tensor_copy",
                              cx.xn[:, c * T + tt * 128: c * T + tt * 128 + 128],
                              cx.pbbf[c % 4][:, 0:128], deps=[tr])
        p.barrier()
    p.fence()
    for k in range(8):
        ld = p.dma("sync", cx.wres[:, k * 2048:(k + 1) * 2048],
                   cx.wq_gate.ap()[k, :, 0:2048])
        p.wait("sync", ld)
    p.barrier()
    gdone = {}
    for half in range(2):
        pass
    p.barrier()
    for strip in range(6):
        w0 = strip * 512
        w1 = min(2816, w0 + 512)
        wn = w1 - w0
        lg_ = p.dma("sync", cx.wres[:, 0:8 * 512].rearrange("p (k c) -> p k c", c=512)[:, :, 0:wn],
                    cx.wq_gate.ap()[:, :, w0:w1].rearrange("k p c -> p k c"))
        lu_ = p.dma("sync", cx.wres[:, 8 * 512:16 * 512].rearrange("p (k c) -> p k c", c=512)[:, :, 0:wn],
                    cx.wq_up.ap()[:, :, w0:w1].rearrange("k p c -> p k c"))
        p.wait("sync", lg_, lu_)
        p.barrier()
        for tt in range(TT128):
            bi = tt % 2
            mmg = mmu = None
            for k in range(8):
                xsl = cx.xn[:, k * T + tt * 128: k * T + tt * 128 + 128]
                dep = [gdone.get((bi, 0))] if k == 0 else []
                mmg = p.op("tensor", "matmul", cx.pb[bi * 2][:, 0:wn], xsl,
                           cx.wres[:, k * 512:k * 512 + wn],
                           start=(k == 0), stop=(k == 7), deps=dep)
                dep = [gdone.get((bi, 1))] if k == 0 else []
                mmu = p.op("tensor", "matmul", cx.pb[bi * 2 + 1][:, 0:wn], xsl,
                           cx.wres[:, 4096 + k * 512:4096 + k * 512 + wn],
                           start=(k == 0), stop=(k == 7), deps=dep)
            sg_ = p.op("scalar", "activation", cx.bbB[:, bi * 2048:bi * 2048 + wn],
                       cx.pb[bi * 2][:, 0:wn], AF.Silu,
                       deps=[mmg, gdone.get((bi, 2))])
            hh = p.op("vector", "tensor_tensor",
                      cx.bbB[:, bi * 2048 + 512:bi * 2048 + 512 + wn],
                      cx.pb[bi * 2 + 1][:, 0:wn],
                      cx.bbB[:, bi * 2048:bi * 2048 + wn], OP.mult,
                      deps=[sg_, mmu, gdone.get((bi, 3))])
            gdone[(bi, 0)] = sg_
            gdone[(bi, 1)] = hh
            gdone[(bi, 2)] = hh
            qsl = {"sc": cx.f512[0][:, bi * 32:bi * 32 + 32],
                   "rc": cx.f512[1][:, bi * 32:bi * 32 + 32],
                   "f1": cx.stage2[:, bi * 1024:bi * 1024 + 512],
                   "f2": cx.stage2[:, bi * 1024 + 512:bi * 1024 + 1024],
                   "b1": cx.bbA[:, bi * 2048:bi * 2048 + 512],
                   "b2": cx.bbA[:, bi * 2048 + 512:bi * 2048 + 1024]}
            dq = emit_fp4_quant(cx, cx.evout[:, bi * 2048:bi * 2048 + wn],
                                cx.bbB[:, bi * 2048 + 512:bi * 2048 + 512 + wn],
                                wn, qsl, deps=[hh, gdone.get((bi, 4))])
            gdone[(bi, 3)] = dq
            gdone[(bi, 4)] = dq
            if w0 < 2048:
                st = p.dma("sync", cx.scrD.ap()[tt, :, w0:w1],
                           cx.evout[:, bi * 2048:bi * 2048 + wn], deps=[dq])
            else:
                st = p.dma("sync", cx.scrD2.ap()[tt, :, w0 - 2048:w1 - 2048],
                           cx.evout[:, bi * 2048:bi * 2048 + wn], deps=[dq])
            gdone[(bi, 0)] = st
        p.fence()
    for tt in range(TT128):
        bi = tt % 2
        ld1 = p.dma("sync", cx.bbB[:, bi * 2048:(bi + 1) * 2048],
                    cx.scrD.ap()[tt, :, 0:2048], deps=[gdone.get(("t2f", bi))])
        ld2 = p.dma("scalar", cx.bbA[:, bi * 2048:bi * 2048 + 768],
                    cx.scrD2.ap()[tt, :, 0:768], deps=[gdone.get(("t2f2", bi))])
        evs = []
        for j in range(22):
            src = (cx.bbB[:, bi * 2048 + j * 128:bi * 2048 + (j + 1) * 128]
                   if j < 16 else
                   cx.bbA[:, bi * 2048 + (j - 16) * 128:bi * 2048 + (j - 15) * 128])
            tr = p.op("tensor", "transpose", cx.pbbf[j % 4][:, 0:128], src,
                      cx.ident[:], deps=[ld1, ld2, gdone.get(("bank", j % 4))])
            cp = p.op("vector", "tensor_copy",
                      cx.evout[:, (j % 8) * 128:(j % 8) * 128 + 128],
                      cx.pbbf[j % 4][:, 0:128],
                      deps=[tr, gdone.get(("ev", j % 8))])
            gdone[("bank", j % 4)] = cp
            st = p.dma("scalar", cx.scrA.ap()[j, :, tt * 128:(tt + 1) * 128],
                       cx.evout[:, (j % 8) * 128:(j % 8) * 128 + 128], deps=[cp])
            gdone[("ev", j % 8)] = st
            evs.append(st)
        gdone[("t2f", bi)] = evs[15]
        gdone[("t2f2", bi)] = evs[21]
    p.fence()
    emit_proj_hbm_act(cx, _DramView(cx.wq_down, 0, 11), cx.scrA, 1408, 1024,
                      make_bias_evac(cx, cx.scrC, dram_off=8), w_bf16=True)
    emit_proj_hbm_act(cx, _DramView(cx.wq_down, 11, 11), cx.scrA, 1408, 1024,
                      make_addhbm_evac(cx, cx.scrC, 8, cx.g_hbm, AF.Silu),
                      act_off=11, w_bf16=True)


def make_addhbm_evac(cx, add_hbm, add_off, out_hbm, act_fn=None, dram_off=0):
    p = cx.p
    outg = [None] * 8
    ldg = [None] * 8

    def evac(o, t, mtok, bank):
        i8 = (o + t) % 8
        out = cx.evout[:, i8 * 512:i8 * 512 + 512]
        add = cx.bbB[:, (i8 % 4) * 512:(i8 % 4) * 512 + 512]
        ld = p.dma("scalar", add, add_hbm.ap()[add_off + o, :, t * 512:(t + 1) * 512],
                   deps=[ldg[i8 % 4]])
        a1 = p.op("vector", "tensor_tensor", out, cx.pb[bank][:], add, OP.add,
                  deps=[mtok, ld, outg[i8]])
        ldg[i8 % 4] = a1
        if act_fn is not None:
            a1 = p.op("scalar", "activation", out, out, act_fn, deps=[a1])
        st = p.dma("sync", out_hbm.ap()[dram_off + o, :, t * 512:(t + 1) * 512],
                   out, deps=[a1])
        outg[i8] = st
        return a1

    return evac


def emit_l0_attn(cx, prep, upto=99):
    p = cx.p
    d = {k: din(cx, k, v) for k, v in prep.items() if k.startswith("l0_")}
    CH = 256
    NCH = SEQ // CH
    H = 32
    dstg = cx.stage.ap().bitcast(F32)
    emit_proj_tm(cx, _DramViewCols(d["l0_wgate"], 0, 1024), cx.xn[:], 1024,
                 1024, cx.scrD, col_lim=2048)
    emit_proj_tm(cx, _DramViewCols(d["l0_wgate"], 1024, 1024), cx.xn[:], 1024,
                 1024, cx.scrD, col_off=1024, col_lim=2048)
    emit_proj_fm2(cx, d["l0_whs"], cx.xn[:], 1024, 2304,
                  make_bias_evac(cx, cx.scrA, dram_off=0))
    emit_proj_fm2(cx, d["l0_wdt"], cx.xn[:], 1024, 128,
                  make_bias_evac(cx, cx.scrA, dram_off=18))
    if upto <= 0:
        return
    ldc = [p.dma("sync", cx.ccols[:, 0:72], d["l0_convw"].ap()),
           p.dma("sync", cx.ccols[:, 72:90], d["l0_convb"].ap())]
    p.wait("sync", *ldc)
    p.barrier()
    cg = {}
    for c in range(18):
        for sq in range(NSEQ):
            bi = (c * NSEQ + sq) % 2
            buf = cx.bbB[:, bi * 2048:(bi + 1) * 2048]
            acc = cx.stage2[:, 0:2048] if bi == 0 else dstg[:, 2048:4096]
            ld = p.dma("sync", buf, cx.scrA.ap()[c, :, sq * SEQ:(sq + 1) * SEQ],
                       deps=[cg.get(("buf", bi))])
            a0 = p.op("vector", "tensor_scalar_mul", acc, buf,
                      cx.ccols[:, 3 * 18 + c:3 * 18 + c + 1],
                      deps=[ld, cg.get(("acc", bi))])
            aprev = a0
            for k in range(3):
                sh = 3 - k
                aprev = p.op("vector", "scalar_tensor_tensor",
                             acc[:, sh:2048], buf[:, 0:2048 - sh],
                             cx.ccols[:, k * 18 + c:k * 18 + c + 1],
                             acc[:, sh:2048], OP.mult, OP.add,
                             deps=[aprev, ld])
            ev = p.op("scalar", "activation", buf, acc, AF.Silu,
                      bias=cx.ccols[:, 72 + c:73 + c], deps=[aprev])
            cg[("acc", bi)] = ev
            if c < 16:
                st = p.dma("sync", cx.scrC.ap()[c, :, sq * SEQ:(sq + 1) * SEQ],
                           buf, deps=[ev])
            else:
                st = p.dma("sync", cx.scrA.ap()[19 + (c - 16), :,
                                                sq * SEQ:(sq + 1) * SEQ],
                           buf, deps=[ev])
            cg[("buf", bi)] = st
    p.fence()
    if upto <= 1:
        return
    ld = p.dma("sync", cx.bbA[0:32, 0:T], cx.scrA.ap()[18, 0:32, :])
    ldb = p.dma("sync", cx.f512[4][0:32, 0:1], d["l0_dtbias"].ap())
    lda = p.dma("sync", cx.f512[4][0:32, 1:2], d["l0_alog"].ap())
    p.wait("vector", ld, ldb, lda)
    z = p.op("vector", "tensor_scalar_add", dstg[0:32, 0:T], cx.bbA[0:32, 0:T],
             cx.f512[4][0:32, 0:1])
    az = p.op("scalar", "activation", cx.bbB[0:32, 0:T], dstg[0:32, 0:T],
              AF.Abs, deps=[z])
    e1 = p.op("scalar", "activation", cx.bbB[0:32, 0:T], cx.bbB[0:32, 0:T],
              AF.Exp, scale=-1.0, deps=[az])
    l1 = p.op("scalar", "activation", cx.bbB[0:32, 0:T], cx.bbB[0:32, 0:T],
              AF.Ln, bias=constcol(cx, 1.0)[0:32, :], deps=[e1])
    r1 = p.op("vector", "tensor_single_scalar", dstg[0:32, 0:T],
              dstg[0:32, 0:T], 0.0, OP.max, deps=[l1, z])
    sp = p.op("vector", "tensor_tensor", dstg[0:32, 0:T], dstg[0:32, 0:T],
              cx.bbB[0:32, 0:T], OP.add, deps=[r1])
    dtp = p.op("vector", "tensor_single_scalar", dstg[0:32, 0:T],
               dstg[0:32, 0:T], 100.0, OP.min, deps=[sp])
    ae = p.op("scalar", "activation", cx.f512[4][0:32, 2:3],
              cx.f512[4][0:32, 1:2], AF.Exp, deps=[lda])
    an = p.op("vector", "tensor_scalar_mul", cx.f512[4][0:32, 2:3],
              cx.f512[4][0:32, 2:3], -1.0, deps=[ae])
    ad = p.op("vector", "tensor_scalar_mul", dstg[32:64, 0:T], dstg[0:32, 0:T],
              cx.f512[4][0:32, 2:3], deps=[an, dtp])
    cur, oth = 64, 96
    prevt = p.op("vector", "tensor_copy", dstg[64:96, 0:T], dstg[32:64, 0:T],
                 deps=[ad])
    for stp in [1, 2, 4, 8, 16, 32, 64, 128]:
        vsrc = dstg[cur:cur + 32, 0:T].rearrange("p (nc ch) -> p nc ch", ch=CH)
        vdst = dstg[oth:oth + 32, 0:T].rearrange("p (nc ch) -> p nc ch", ch=CH)
        t1 = p.op("vector", "tensor_tensor", vdst[:, :, stp:CH],
                  vsrc[:, :, stp:CH], vsrc[:, :, 0:CH - stp], OP.add,
                  deps=[prevt])
        t2 = p.op("vector", "tensor_copy", vdst[:, :, 0:stp], vsrc[:, :, 0:stp],
                  deps=[prevt])
        prevt = p.op("vector", "tensor_copy", dstg[oth:oth + 1, 0:1],
                     dstg[oth:oth + 1, 0:1], deps=[t1, t2])
        cur, oth = oth, cur
    cb = p.dma("sync", cx.acs_hbm.ap(), dstg[cur:cur + 32, 0:T], deps=[prevt])
    onesf = cx.f512[4][0:1, 64:192]
    onf = p.op("vector", "memset", onesf, 1.0)
    tg = {}
    id32 = cx.identf[0:32, 0:32]
    idc = cx.identf[cur:cur + 32, cur:cur + 32]
    for tt in range(TT128):
        tr1 = p.op("tensor", "transpose", cx.pbm[:, 0:32],
                   dstg[cur:cur + 32, tt * 128:(tt + 1) * 128], idc,
                   deps=[prevt, tg.get("m")])
        cp1 = p.op("vector", "tensor_copy", cx.acsT[:, tt * 32:(tt + 1) * 32],
                   cx.pbm[:, 0:32], deps=[tr1, tg.get("q0")])
        tg["m2c"] = cp1
        ng1 = p.op("gpsimd", "tensor_scalar_mul",
                   cx.brow[:, tt * 32:(tt + 1) * 32],
                   cx.acsT[:, tt * 32:(tt + 1) * 32], -1.0, deps=[cp1])
        tr2 = p.op("tensor", "transpose", cx.pbm[:, 32:64],
                   dstg[0:32, tt * 128:(tt + 1) * 128], id32,
                   deps=[dtp, tg.get("m2"), cp1])
        cp2 = p.op("vector", "tensor_copy", cx.att_k[0][:, tt * 32:(tt + 1) * 32],
                   cx.pbm[:, 32:64], deps=[tr2, tg.get("k0")])
        tg["m"] = cp2
        tg["m2"] = cp2
        tg["q0"] = ng1
        tg["k0"] = cp2
    ldd = p.dma("sync", cx.stage2[0:1, 0:32], d["l0_drow"].ap())
    cdd = p.op("vector", "tensor_copy", cx.bbA[0:1, 0:32], cx.stage2[0:1, 0:32],
               deps=[ldd])
    mdd = p.op("tensor", "matmul", cx.pb[2][:, 0:32], cx.ones[0:1, :],
               cx.bbA[0:1, 0:32], start=True, stop=True, deps=[cdd])
    cdd2 = p.op("vector", "tensor_copy", cx.att_v[0][:, 0:32], cx.pb[2][:, 0:32],
                deps=[mdd])
    ltri = [p.dma("sync", cx.att_p[0][0:128, 0:256], d["l0_triu0"].ap()),
            p.dma("sync", cx.att_p[1][0:128, 0:256], d["l0_triu1"].ap())]
    p.wait("sync", *ltri)
    p.fence()

    if upto <= 2:
        return
    W = cx.wres
    prevS = W[:, 0:2048]
    stev = W[:, 2048:4096]
    gt_sl = [W[:, 8192 + i * 256:8192 + (i + 1) * 256] for i in range(2)]
    mt_sl = [W[:, 8704 + i * 256:8704 + (i + 1) * 256] for i in range(4)]
    csc_sl = [W[:, 9728 + i * 256:9728 + (i + 1) * 256] for i in range(2)]
    ea_sl = [W[:, 10240 + i * 256:10240 + (i + 1) * 256] for i in range(2)]
    g = {}
    for sq in range(NSEQ):
        lB = p.dma("sync", cx.bbA[:, 0:2048],
                   cx.scrA.ap()[19, :, sq * SEQ:(sq + 1) * SEQ],
                   deps=[g.get("Bg")])
        lC = p.dma("sync", cx.bbA[:, 2048:4096],
                   cx.scrA.ap()[20, :, sq * SEQ:(sq + 1) * SEQ],
                   deps=[g.get("Bg")])
        zS = p.op("vector", "memset", prevS, 0.0, deps=[g.get("pSg")])
        g["pS"] = zS
        for c in range(NCH):
            coff = sq * SEQ + c * CH
            ti0 = coff // 128
            stageb2 = cx.stage2.ap().bitcast(BF16)
            for j in range(16):
                lx = p.dma("scalar", stageb2[:, (j % 8) * 512:(j % 8) * 512 + 256],
                           cx.scrC.ap()[j, :, coff:coff + 256],
                           deps=[g.get(("xld", j % 8))])
                for lh in range(2):
                    trx = p.op("tensor", "transpose", cx.pbbf[4 + (j % 2)][:, 0:128],
                               stageb2[:, (j % 8) * 512 + lh * 128:
                                       (j % 8) * 512 + lh * 128 + 128],
                               cx.ident[:],
                               deps=[lx, g.get(("xtb", j % 2)),
                                     g.get(("sbank", j % 2))])
                    cpx = p.op("vector", "tensor_copy",
                               cx.evout[:, lh * 2048 + j * 128:
                                        lh * 2048 + (j + 1) * 128],
                               cx.pbbf[4 + (j % 2)][:, 0:128],
                               deps=[trx, g.get(("XT", lh, j))])
                    g[("xtb", j % 2)] = cpx
                    g[("XT", lh, j)] = cpx
                g[("xld", j % 8)] = g[("xtb", j % 2)]
            xst = []
            for lh in range(2):
                dtsl = cx.att_k[0][:, (ti0 + lh) * 32:(ti0 + lh) * 32 + 32]
                dep = [g[("XT", lh, j)] for j in range(16)]
                xs1 = p.op("vector", "tensor_tensor",
                           cx.bbB[:, lh * 2048:(lh + 1) * 2048]
                           .rearrange("p (h v) -> p h v", v=64),
                           cx.evout[:, lh * 2048:(lh + 1) * 2048]
                           .rearrange("p (h v) -> p h v", v=64),
                           dtsl.unsqueeze(2).to_broadcast([128, 32, 64]),
                           OP.mult, deps=dep + [g.get(("xsg", lh)), cp2])
                xst.append(xs1)
            gmm = []
            for sh in range(2):
                mG = p.op("tensor", "matmul", cx.pb[2 + sh][:, 0:256],
                          cx.bbA[:, c * 256 + sh * 128:c * 256 + sh * 128 + 128],
                          cx.bbA[:, 2048 + c * 256:2048 + (c + 1) * 256],
                          start=True, stop=True,
                          deps=[lB, lC, g.get(("gtb", sh)), g.get("pb2read"),
                                g.get("pb2read2")])
                mk = p.op("vector", "tensor_tensor", gt_sl[sh][:],
                          cx.pb[2 + sh][:, 0:256],
                          cx.att_p[sh][0:128, 0:256], OP.mult,
                          deps=[mG, g.get(("gts", sh))])
                g[("gtb", sh)] = mk
                gmm.append(mk)
            lcol = p.op("tensor", "transpose", cx.pbm[0:1, 64:96],
                        dstg[cur:cur + 32, coff + 255:coff + 256], idc,
                        deps=[prevt, g.get("lrow"), g.get(("sbank", 3))])
            crow = p.op("vector", "tensor_copy", cx.f512[4][0:1, 8:40],
                        cx.pbm[0:1, 64:96], deps=[lcol, g.get("crow")])
            g["lrow"] = crow
            bcL = p.op("tensor", "matmul", cx.pb[2][:, 256:288], onesf,
                       cx.f512[4][0:1, 8:40], start=True, stop=True,
                       deps=[crow, onf, g.get("bcLg"), gmm[0]])
            lam = p.op("scalar", "activation", cx.f512[3][:, 0:32],
                       cx.pb[2][:, 256:288], AF.Exp, deps=[bcL, g.get("lamg")])
            dcy = []
            for sh in range(2):
                ds = p.op("vector", "tensor_tensor", cx.f512[2][:, sh * 32:sh * 32 + 32],
                          cx.pb[2][:, 256:288],
                          cx.acsT[:, (ti0 + sh) * 32:(ti0 + sh) * 32 + 32],
                          OP.subtract, deps=[bcL, g.get(("dcyg", sh))])
                de = p.op("scalar", "activation", cx.f512[2][:, sh * 32:sh * 32 + 32],
                          cx.f512[2][:, sh * 32:sh * 32 + 32], AF.Exp, deps=[ds])
                dcy.append(de)
                g[("dcyg", sh)] = de
            g["bcLg"] = dcy[1]
            g["pb2read"] = dcy[1]
            g["pb2read2"] = lam
            csc_tok = {}
            mt_tok = {}
            for h in range(H):
                rsl = (cx.f512[4][0:1, 192:448] if h % 2 == 0
                       else cx.f512[2][0:1, 64:320])
                lr = p.dma("sync", rsl, cx.acs_hbm.ap()[h, coff:coff + 256],
                           deps=[cb, g.get(("rsl", h % 2))])
                bcA = p.op("tensor", "matmul", cx.pb[6][:, 0:256],
                           onesf, rsl, start=True, stop=True,
                           deps=[lr, onf, g.get("bcag1"), g.get("mtb"),
                                 g.get(("sbank", 2))])
                g[("rsl", h % 2)] = bcA
                eaA = p.op("scalar", "activation", ea_sl[h % 2][:],
                           cx.pb[6][:, 0:256], AF.Exp,
                           deps=[bcA, g.get(("eag", h % 2))])
                cs = p.op("vector", "tensor_tensor", csc_sl[h % 2][:],
                          cx.bbA[:, 2048 + c * 256:2048 + (c + 1) * 256],
                          ea_sl[h % 2][:], OP.mult,
                          deps=[eaA, lC, csc_tok.get(h % 2)])
                g[("eag", h % 2)] = cs
                for sh in range(2):
                    mi = 2 * (h % 2) + sh
                    clM = p.op("vector", "tensor_scalar", cx.pb[7][:, 64:320],
                               cx.pb[6][:, 0:256],
                               cx.brow[:, (ti0 + sh) * 32 + h:
                                       (ti0 + sh) * 32 + h + 1],
                               0.0, OP.add, OP.min,
                               deps=[bcA, g.get("mtb"), crow])
                    exM = p.op("scalar", "activation", mt_sl[mi][:],
                               cx.pb[7][:, 64:320], AF.Exp,
                               deps=[clM, mt_tok.get(mi)])
                    g["mtb"] = exM
                    mm2 = p.op("vector", "tensor_tensor", mt_sl[mi][:],
                               mt_sl[mi][:], gt_sl[sh][:], OP.mult,
                               deps=[exM, gmm[sh]])
                    mt_tok[mi] = mm2
                    g["bcag1"] = clM
                for lh in range(2):
                    bank = cx.pb[lh]
                    col = (h % 8) * 64
                    first = (h % 8 == 0)
                    mm1 = p.op("tensor", "matmul", bank[:, col:col + 64],
                               mt_sl[2 * (h % 2)][:, lh * 128:lh * 128 + 128],
                               cx.bbB[:, 0 * 2048 + h * 64:0 * 2048 + h * 64 + 64],
                               start=first, stop=False, force_tok=True,
                               deps=[mt_tok[2 * (h % 2)], xst[0]] +
                                    ([g.get(("ybank", lh, h // 8))] if first else []))
                    mm2_ = p.op("tensor", "matmul", bank[:, col:col + 64],
                                mt_sl[2 * (h % 2) + 1][:, lh * 128:lh * 128 + 128],
                                cx.bbB[:, 1 * 2048 + h * 64:1 * 2048 + h * 64 + 64],
                                start=False, stop=False, force_tok=True,
                                deps=[mt_tok[2 * (h % 2) + 1], xst[1]])
                    mmo = p.op("tensor", "matmul", bank[:, col:col + 64],
                               csc_sl[h % 2][:, lh * 128:lh * 128 + 128],
                               prevS[:, h * 64:h * 64 + 64],
                               start=False, stop=(h % 8 == 7), force_tok=True,
                               deps=[cs, g["pS"]])
                    mt_tok[2 * (h % 2)] = mm1
                    mt_tok[2 * (h % 2) + 1] = mm2_
                    csc_tok[h % 2] = mmo
                    if h % 8 == 7:
                        hg = h // 8
                        xsl = cx.evout[:, lh * 2048 + hg * 512:
                                       lh * 2048 + hg * 512 + 512]
                        tmp = cx.att_p[2][:, 0:512]
                        dsl = cx.att_v[0][:, hg * 8:hg * 8 + 8]
                        t1_ = p.op("vector", "tensor_tensor",
                                   tmp.rearrange("p (hh v) -> p hh v", v=64),
                                   xsl.rearrange("p (hh v) -> p hh v", v=64),
                                   dsl.unsqueeze(2).to_broadcast([128, 8, 64]),
                                   OP.mult, deps=[mmo, g.get("t1g"), cdd2] +
                                   [g[("XT", lh, hg * 4 + q_)] for q_ in range(4)])
                        y1 = p.op("vector", "tensor_tensor",
                                  cx.att_o[:, 0:512] if False else cx.att_k[1][:, 0:512],
                                  bank[:, 0:512], tmp, OP.add,
                                  deps=[t1_, mmo, g.get("y1g")])
                        g["t1g"] = y1
                        st = p.dma("sync",
                                   cx.scrD2.ap()[ti0 + lh, :, hg * 512:hg * 512 + 512],
                                   cx.att_k[1][:, 0:512], deps=[y1])
                        g["y1g"] = st
                        g[("ybank", lh, hg)] = y1
            btk = []
            for sh in range(2):
                trb = p.op("tensor", "transpose", cx.pbbf[6][:, 0:128],
                           cx.bbA[:, c * 256 + sh * 128:c * 256 + sh * 128 + 128],
                           cx.ident[:], deps=[lB, g.get("btg")])
                cpb = p.op("vector", "tensor_copy",
                           W[:, 12288 + sh * 128:12288 + sh * 128 + 128],
                           cx.pbbf[6][:, 0:128], deps=[trb, g.get(("btc", sh))])
                g["btg"] = cpb
                g[("btc", sh)] = cpb
                btk.append(cpb)
            xsd = []
            for sh in range(2):
                xd = p.op("vector", "tensor_tensor",
                          cx.evout[:, sh * 2048:(sh + 1) * 2048]
                          .rearrange("p (h v) -> p h v", v=64),
                          cx.bbB[:, sh * 2048:(sh + 1) * 2048]
                          .rearrange("p (h v) -> p h v", v=64),
                          cx.f512[2][:, sh * 32:sh * 32 + 32]
                          .unsqueeze(2).to_broadcast([128, 32, 64]),
                          OP.mult,
                          deps=[dcy[sh], xst[sh], g.get("y1g"), g.get("t1g")])
                xsd.append(xd)
                g[("xsg", sh)] = xd
            smm = {}
            for h in range(H):
                bank = cx.pb[4 + h // 8]
                col = (h % 8) * 64
                m1 = p.op("tensor", "matmul", bank[:, col:col + 64],
                          W[:, 12288:12288 + 128],
                          cx.evout[:, 0 * 2048 + h * 64:0 * 2048 + h * 64 + 64],
                          start=(h % 8 == 0), stop=False,
                          deps=[btk[0], xsd[0]] +
                               ([g.get(("sbank", h // 8)), g.get("mtb")]
                                if h % 8 == 0 else []))
                m2 = p.op("tensor", "matmul", bank[:, col:col + 64],
                          W[:, 12288 + 128:12288 + 256],
                          cx.evout[:, 1 * 2048 + h * 64:1 * 2048 + h * 64 + 64],
                          start=False, stop=(h % 8 == 7),
                          deps=[btk[1], xsd[1]])
                smm[h // 8] = m2
            sev = []
            for q in range(4):
                se = p.op("vector", "tensor_copy", stev[:, q * 512:(q + 1) * 512],
                          cx.pb[4 + q][:], deps=[smm[q], g.get(("sevg", q))])
                sev.append(se)
                g[("sbank", q)] = se
                g[("sevg", q)] = se
            u1 = p.op("vector", "tensor_tensor",
                      prevS.rearrange("p (h v) -> p h v", v=64),
                      prevS.rearrange("p (h v) -> p h v", v=64),
                      cx.f512[3][:, 0:32].unsqueeze(2).to_broadcast([128, 32, 64]),
                      OP.mult, deps=[lam] + sev + [g["pS"]])
            u2 = p.op("vector", "tensor_tensor", prevS, prevS, stev, OP.add,
                      deps=[u1])
            g["pS"] = u2
            g["lamg"] = u1
        g["Bg"] = g.get("btg")
        g["pSg"] = g.get("pS")
    p.fence()
    if upto <= 3:
        return
    lnw = p.dma("sync", cx.stage2[0:1, 0:2048], d["l0_normw"].ap())
    cnw = p.op("vector", "tensor_copy", cx.bbA[0:1, 0:2048],
               cx.stage2[0:1, 0:2048], deps=[lnw])
    nbc = []
    for nb in range(4):
        m = p.op("tensor", "matmul", cx.pb[4 + nb][:], cx.ones[0:1, :],
                 cx.bbA[0:1, nb * 512:(nb + 1) * 512], start=True, stop=True,
                 deps=[cnw])
        nbc.append(p.op("vector", "tensor_copy",
                        cx.brow[:, 0:1024] if False else cx.att_q[0][:, nb * 512:(nb + 1) * 512],
                        cx.pb[4 + nb][:], deps=[m]))
    p.fence()
    eps_col = constcol(cx, 1e-5)
    gg = {}
    for tt in range(TT128):
        bi = tt % 2
        ly = p.dma("sync", cx.bbB[:, bi * 2048:(bi + 1) * 2048],
                   cx.scrD2.ap()[tt], deps=[gg.get(("y", bi))])
        lg_ = p.dma("scalar", cx.bbA[:, bi * 2048:(bi + 1) * 2048],
                    cx.scrD.ap()[tt], deps=[gg.get(("g", bi))])
        sg_ = p.op("scalar", "activation", cx.evout[:, bi * 2048:(bi + 1) * 2048],
                   cx.bbA[:, bi * 2048:(bi + 1) * 2048], AF.Silu,
                   deps=[lg_, gg.get(("sgv", bi))] + gg.get(("evlist", bi), []))
        stgf = cx.stage2[:, 0:2048] if bi == 0 else dstg[:, 2048:4096]
        yg = p.op("vector", "tensor_tensor", stgf,
                  cx.bbB[:, bi * 2048:(bi + 1) * 2048],
                  cx.evout[:, bi * 2048:(bi + 1) * 2048], OP.mult,
                  deps=[ly, sg_, gg.get(("yg", bi))])
        gg[("sgv", bi)] = yg
        zac = p.op("vector", "memset", cx.f512[4][:, 3 + bi:4 + bi], 0.0,
                   deps=[gg.get(("zac", bi))])
        sq_ = p.op("scalar", "activation", cx.evout[:, bi * 2048:(bi + 1) * 2048],
                   stgf, AF.Square, accum_out=cx.f512[4][:, 3 + bi:4 + bi],
                   deps=[yg, zac, gg.get(("sq", bi))])
        v1 = p.op("vector", "tensor_scalar_mul", cx.f512[4][:, 3 + bi:4 + bi],
                  cx.f512[4][:, 3 + bi:4 + bi], 1.0 / 2048.0, deps=[sq_])
        gg[("zac", bi)] = v1
        v2 = p.op("scalar", "activation", cx.f512[4][:, 3 + bi:4 + bi],
                  cx.f512[4][:, 3 + bi:4 + bi], AF.Ln, bias=eps_col[:],
                  deps=[v1])
        v3 = p.op("scalar", "activation", cx.f512[4][:, 3 + bi:4 + bi],
                  cx.f512[4][:, 3 + bi:4 + bi], AF.Exp, scale=-0.5, deps=[v2])
        dbgtok = None
        if tt == 0 and hasattr(cx, "dbg_yg"):
            dbgtok = p.dma("sync", cx.dbg_yg.ap()[0], stgf, deps=[sq_])
        yn = p.op("vector", "tensor_scalar_mul", stgf, stgf,
                  cx.f512[4][:, 3 + bi:4 + bi], deps=[v3, sq_, dbgtok])
        yn2 = p.op("vector", "tensor_tensor",
                   cx.bbB[:, bi * 2048:(bi + 1) * 2048], stgf,
                   cx.att_q[0][:, 0:2048], OP.mult, deps=[yn] + nbc)
        gg[("yg", bi)] = yn2
        gg[("sq", bi)] = yn2
        evs = []
        for j in range(16):
            trj = p.op("tensor", "transpose", cx.pbbf[j % 4][:, 0:128],
                       cx.bbB[:, bi * 2048 + j * 128:bi * 2048 + (j + 1) * 128],
                       cx.ident[:], deps=[yn2, gg.get(("tb", j % 4))])
            cpj = p.op("vector", "tensor_copy",
                       cx.evout[:, (j % 8) * 128 + (bi % 1) * 0:
                                (j % 8) * 128 + 128] if False else
                       cx.evout[:, bi * 2048 + (j % 16) * 128:
                                bi * 2048 + (j % 16) * 128 + 128],
                       cx.pbbf[j % 4][:, 0:128],
                       deps=[trj, gg.get(("ev", bi, j))])
            gg[("tb", j % 4)] = cpj
            stj = p.dma("scalar", cx.scrA.ap()[j, :, tt * 128:(tt + 1) * 128],
                        cx.evout[:, bi * 2048 + j * 128:
                                 bi * 2048 + (j + 1) * 128], deps=[cpj])
            gg[("ev", bi, j)] = stj
            evs.append(stj)
        gg[("y", bi)] = cpj
        gg[("g", bi)] = sg_
        gg[("evlist", bi)] = evs
    p.fence()
    emit_proj_hbm_act(cx, d["l0_outw"], cx.scrA, 2048, 1024,
                      make_bias_evac(cx, cx.s_hbm))


class _DramViewCols:
    def __init__(self, h, c0, n):
        self.h = h
        self.c0 = c0
        self.n = n

    def ap(self):
        return self.h.ap()[:, :, self.c0:self.c0 + self.n]


def emit_combine(cx, y_out=None):
    p = cx.p
    xs = cx.stage
    sg = cx.bbA
    st = cx.bbB
    prev = []
    for t in range(TT512):
        t0 = t * 512
        for c in range(NC):
            slx = slice(c * 512, (c + 1) * 512)
            lx = p.dma("sync", xs[:, slx], cx.x_hbm.ap()[c, :, t0:t0 + 512],
                       deps=prev)
            ls = p.dma("sync", st[:, slx], cx.s_hbm.ap()[c, :, t0:t0 + 512])
            lg = p.dma("sync", sg[:, slx], cx.g_hbm.ap()[c, :, t0:t0 + 512])
            eng = "vector" if c % 2 == 0 else "gpsimd"
            m1 = p.op(eng, "tensor_tensor", sg[:, slx], st[:, slx], sg[:, slx],
                      OP.mult, deps=[ls, lg])
            m2 = p.op(eng, "tensor_tensor", xs[:, slx], xs[:, slx], sg[:, slx],
                      OP.add, deps=[m1, lx])
            wb = p.dma("scalar", cx.x_hbm.ap()[c, :, t0:t0 + 512], xs[:, slx],
                       deps=[m2])
            prev = [wb]
            if y_out is not None:
                wy = p.dma("scalar", y_out.ap()[c, :, t0:t0 + 512], xs[:, slx],
                           deps=[m2])
                prev.append(wy)
    p.fence()


CONST_DECLS = [("c_ones", [128, 128], BF16),
               ("c_ident", [128, 128], BF16),
               ("c_identf", [128, 128], F32),
               ("c_ltri_neg", [128, 128], F32),
               ("c_cmask0", [128, 512], BF16),
               ("c_cmask1", [128, 512], BF16),
               ("c_cmask2", [128, 512], BF16),
               ("c_cmask3", [128, 512], BF16)]


def build_program(prep):
    p = Prog()
    x_in = p.dram("x_in", [NC, 128, T], F32, kind="ExternalInput")
    y_out = p.dram("y", [NC, 128, T], F32, kind="ExternalOutput")
    consts = {}
    for name, shape, dt in CONST_DECLS:
        consts[name] = p.dram(name, shape, dt, kind="ExternalInput").ap()
    cx = build_common(p)
    load_consts(cx, consts)

    tk = [p.dma("sync", cx.x_hbm.ap()[c], x_in.ap()[c]) for c in range(NC)]
    p.wait("sync", *tk)
    p.fence()

    emit_l0_quant_weights(cx, prep)

    emit_ln(cx, cx.x_hbm, cx.xn[:])
    emit_l0_attn(cx, prep)
    emit_l0_ffn(cx, prep)
    emit_combine(cx)
    emit_ln(cx, cx.x_hbm, cx.xn[:])
    emit_l1_attn(cx, prep)
    emit_l1_ffn(cx, prep)
    emit_combine(cx)
    emit_ln(cx, cx.x_hbm, cx.xn[:])
    emit_l2_attn(cx, prep)
    emit_l2_ffn(cx, prep)
    emit_combine(cx)
    emit_ln(cx, cx.x_hbm, cx.xn[:])
    emit_l3_attn(cx, prep)
    emit_l3_ffn(cx, prep)
    emit_combine(cx, y_out=y_out)
    return p


def kernel(**inputs):
    percore, shared = prep_inputs(inputs)
    g = {k: np.asarray(v) for k, v in inputs.items()}
    prep = {}
    prep.update(prep_l0(g))
    prep.update(prep_l1_attn(g))
    prep.update(prep_l1_ffn(g))
    prep.update(prep_l2_attn(g))
    prep.update(prep_l2_ffn(g))
    prep.update(prep_l3(g))
    prep.update(prep_l3_ffn(g))
    p = build_program(prep)
    in_maps = []
    for c in range(NCORE):
        m = {"x_in": percore[c]["x_in"]}
        m.update(shared)
        m.update(prep)
        in_maps.append(m)
    res = run_bass_kernel_spmd(p.nc, in_maps, core_ids=list(range(NCORE)))
    outs = []
    for c in range(NCORE):
        y = res.results[c]["y"]
        outs.append(y.reshape(D, T).T.reshape(NSEQ, SEQ, D))
    return np.concatenate(outs, axis=0).astype(np.float32)
